# version 03
# baseline (speedup 1.0000x reference)
import math
import os
import numpy as np
import ml_dtypes
from contextlib import ExitStack
import concourse.bass as bass
import concourse.mybir as mybir
from concourse.bass_core_rust import IndirectOffsetOnAxis
from concourse.bass_utils import run_bass_kernel_spmd

F32 = mybir.dt.float32
F16 = mybir.dt.float16
BF16 = mybir.dt.bfloat16
I32 = mybir.dt.int32
AF = mybir.ActivationFunctionType
OP = mybir.AluOpType
AX = mybir.AxisListType
NPBF = ml_dtypes.bfloat16

BB, S, D = 2, 4096, 4096
H, G, DH = 32, 8, 128
HID = 11008
HC = 86
NSL_LO, NSL_HI = 16, 32
NSLOT = 48
SCALE = 1.0 / math.sqrt(128.0)
SHIFT = 12.0
NEGB = -1e30
EPS = 1e-5
UP_N = [11, 11, 11, 11, 11, 11, 10, 10]
UP_PAD = 11
NF4 = [-1.0, -0.6961928009986877, -0.5250730514526367, -0.39491748809814453,
       -0.28444138169288635, -0.18477343022823334, -0.09105003625154495, 0.0,
       0.07958029955625534, 0.16093020141124725, 0.24611230194568634,
       0.33791524171829224, 0.44070982933044434, 0.5626170039176941,
       0.7229568362236023, 1.0]


def up_glob(r, local):
    return sum(UP_N[:r]) + local


ENGS = ("sync", "gpsimd", "vector", "scalar", "tensor")
ESEM = {"sync": (None, "LD"), "gpsimd": ("PO", "LG"), "vector": ("DV", None),
        "scalar": ("AC", "LA"), "tensor": ("PE", None)}
ALL_SEMS = ["LD", "LG", "LA", "DV", "AC", "PE", "PO", "CC", "OT"]
COMPUTE_SEMS = {"DV": "vector", "AC": "scalar", "PE": "tensor", "PO": "gpsimd"}


class Res:
    __slots__ = ("name", "prod", "readers", "init")

    def __init__(self, name, init=None):
        self.name = name
        self.prod = None
        self.readers = {}
        self.init = dict(init) if init else {}


class Bld:
    def __init__(self):
        self.q = {e: [] for e in ENGS}
        self.ctr = {s: 0 for s in ALL_SEMS}
        self.hw = {e: {s: 0 for s in ALL_SEMS} for e in ENGS}

    def step(self, eng, sem, inc, reads, writes, body):
        waits = {}

        def merge(tok):
            if tok is None:
                return
            s, c = tok
            if c > waits.get(s, 0):
                waits[s] = c

        for r in reads:
            merge(r.prod)
            for s, c in r.init.items():
                merge((s, c))
        for r in writes:
            merge(r.prod)
            for s, c in r.readers.items():
                merge((s, c))
            for s, c in r.init.items():
                merge((s, c))
        emit_w = {}
        for s, c in waits.items():
            if COMPUTE_SEMS.get(s) == eng:
                continue
            if c <= self.hw[eng][s]:
                continue
            emit_w[s] = c
            self.hw[eng][s] = c
        self.ctr[sem] += inc
        cnt = self.ctr[sem]
        self.q[eng].append((emit_w, body, sem, inc))
        tok = (sem, cnt)
        for r in reads:
            if cnt > r.readers.get(sem, 0):
                r.readers[sem] = cnt
        for r in writes:
            r.prod = tok
            r.readers = {}
            r.init = {}
        return tok

    def snapshot(self):
        return dict(self.ctr)


KTRUNC = int(os.environ.get("KTRUNC", "99"))
KT2 = int(os.environ.get("KT2", "99"))
KSW = os.environ.get("KSW", "")
KT6 = int(os.environ.get("KT6", "99"))
KD = os.environ.get("KD", "")
KDBG = os.environ.get("KDBG", "") == "1"


class _Done(Exception):
    pass


def build_nc():
    nc = bass.Bass("TRN2", target_bir_lowering=False)
    dt = nc.dram_tensor

    x_own = dt("x_own", [1024, D], F32, kind="ExternalInput").ap()
    x_ownT = dt("x_ownT", [D, 1024], F32, kind="ExternalInput").ap()
    Wq = dt("Wq", [D, D], F32, kind="ExternalInput").ap()
    Wk = dt("Wk", [D, 1024], F32, kind="ExternalInput").ap()
    Wv = dt("Wv", [D, 1024], F32, kind="ExternalInput").ap()
    Wo = dt("Wo", [D, D], F32, kind="ExternalInput").ap()
    bq_i = dt("bq_i", [128, 32], F32, kind="ExternalInput").ap()
    bk_i = dt("bk_i", [128, 8], F32, kind="ExternalInput").ap()
    bv_i = dt("bv_i", [1, 1024], F32, kind="ExternalInput").ap()
    bo_i = dt("bo_i", [128, 32], F32, kind="ExternalInput").ap()
    bd_i = dt("bd_i", [128, 32], F32, kind="ExternalInput").ap()
    bu_i = dt("bu_i", [128, HC], F32, kind="ExternalInput").ap()
    cosq = dt("cosq", [128, 1024], BF16, kind="ExternalInput").ap()
    sinq = dt("sinq", [128, 1024], BF16, kind="ExternalInput").ap()
    masks_i = dt("masks_i", [128, 4 * 512], BF16, kind="ExternalInput").ap()
    ident_i = dt("ident_i", [128, 128], BF16, kind="ExternalInput").ap()
    btbl_i = dt("btbl_i", [128, NSLOT], F32, kind="ExternalInput").ap()
    kidx_i = dt("kidx_i", [128, G * NSLOT], I32, kind="ExternalInput").ap()
    cod_u = dt("cod_u", [UP_PAD * 128, D], I32, kind="ExternalInput").ap()
    amx_u = dt("amx_u", [UP_PAD * 128, 64], F32, kind="ExternalInput").ap()
    cod_d = dt("cod_d", [512, HID], I32, kind="ExternalInput").ap()
    amx_d = dt("amx_d", [512, 172], F32, kind="ExternalInput").ap()
    la_u = dt("la_u", [D, 16], F32, kind="ExternalInput").ap()
    lb_u = dt("lb_u", [16, HID], F32, kind="ExternalInput").ap()
    la_d = dt("la_d", [HID, 16], F32, kind="ExternalInput").ap()
    lb_d = dt("lb_d", [16, D], F32, kind="ExternalInput").ap()
    out_fm = dt("out_fm", [2 * D, 512], F32, kind="ExternalOutput").ap()

    if KDBG:
        dbg_kg = dt("dbg_kg", [4 * G * 8 * 128, 128], BF16, kind="ExternalOutput").ap()
        dbg_vg = dt("dbg_vg", [4 * G * 8 * 128, 128], BF16, kind="ExternalOutput").ap()
        dbg_ctx = dt("dbg_ctx", [H * 128, 1024], BF16, kind="ExternalOutput").ap()
        dbg_atn = dt("dbg_atn", [D, 1024], F32, kind="ExternalOutput").ap()
        dbg_wu = dt("dbg_wu", [UP_PAD * 128, D], BF16, kind="ExternalOutput").ap()
        dbg_xn = dt("dbg_xn", [128, 32 * 512], BF16, kind="ExternalOutput").ap()
        dbg_x2 = dt("dbg_x2", [128, HC * 512], BF16, kind="ExternalOutput").ap()
        dbg_tl = dt("dbg_tl", [16, 1024], BF16, kind="ExternalOutput").ap()
        dbg_st = dt("dbg_st", [128, 8], F32, kind="ExternalOutput").ap()
        dbg_kps = dt("dbg_kps", [128, 512], F32, kind="ExternalOutput").ap()
    k_own = dt("k_own", [G * 8 * 128, 128], BF16)
    v_own = dt("v_own", [G * 8 * 128, 128], BF16)
    k_gth = dt("k_gth", [4 * G * 8 * 128, 128], BF16)
    v_gth = dt("v_gth", [4 * G * 8 * 128, 128], BF16)
    wu_own = dt("wu_own", [UP_PAD * 128, D], BF16)
    wd_own = dt("wd_own", [512, HID], BF16)
    wu_gth = dt("wu_gth", [8 * UP_PAD * 128, D], BF16)
    wd_gth = dt("wd_gth", [D, HID], BF16)
    ctx_dr = dt("ctx_dr", [H * 128, 1024], BF16)
    atn_dr = dt("atn_dr", [D, 1024], F32)

    b = Bld()

    with ExitStack() as ctx:
        sem_h = {s: ctx.enter_context(nc.semaphore(s)) for s in ALL_SEMS}
        HEAPB = 206 * 1024
        heap_t = ctx.enter_context(nc.sbuf_tensor("heap", [128, HEAPB // 2], BF16))
        heap = heap_t[:]
        cur = [0]

        def carve(nbytes, align=128):
            o = (cur[0] + align - 1) // align * align
            cur[0] = o + nbytes
            assert cur[0] <= HEAPB, f"heap overflow at {cur[0]}"
            return o

        def view(off, parts, n, dtyp):
            sz = {F32: 4, F16: 2, BF16: 2, I32: 4}[dtyp]
            v = heap[0:parts, off // 2: (off + n * sz) // 2]
            if dtyp != BF16:
                v = v.bitcast(dtyp)
            return v

        o_cos = carve(2048); o_sin = carve(2048)
        o_idn = carve(256); o_btb = carve(NSLOT * 4)
        o_kidx = carve(G * NSLOT * 4)
        o_bq = carve(128); o_bk = carve(32); o_bv = carve(2048)
        o_bo = carve(128); o_bd = carve(128); o_bu = carve(HC * 4)
        o_bf = carve(128)
        o_on1 = carve(4); o_on2 = carve(256); o_on3 = carve(512)
        o_eps = carve(128)
        o_lau = carve(32 * 16 * 2); o_lad = carve(HC * 16 * 2)
        o_st = carve(32)
        o_wgr = [carve(1024) for _ in range(8)]
        o_tl = carve(1024); o_t2 = carve(1024)
        o_lbu = [carve(1024) for _ in range(2)]
        o_lbd = [carve(1024) for _ in range(2)]
        o_xna = carve(32 * 1024)
        o_xnb = carve(32 * 1024)
        o_ph = carve(0)
        PH_SZ = HEAPB - o_ph
        assert PH_SZ >= 101000, f"phase area only {PH_SZ}"
        o_rt = [o_ph + 80 * 1024, o_ph + 82 * 1024]
        o_cev = o_ph + 84 * 1024
        o_qbf = [o_ph + 52 * 1024 + i * 2048 for i in range(4)]
        o_ptl = [o_ph + 60 * 1024 + i * 2048 for i in range(4)]
        o_rrp = o_ph + 68 * 1024
        o_sum = o_ph + 72 * 1024
        o_rin = o_ph + 76 * 1024
        o_fev = [o_ph + 92160 + i * 2048 for i in range(4)]

        cos_s = view(o_cos, 128, 1024, BF16)
        sin_s = view(o_sin, 128, 1024, BF16)
        idn_s = view(o_idn, 128, 128, BF16)
        btb_s = view(o_btb, 128, NSLOT, F32)
        kidx_s = view(o_kidx, 128, G * NSLOT, I32)
        bq_s = view(o_bq, 128, 32, F32)
        bk_s = view(o_bk, 128, 8, F32)
        bv_s = view(o_bv, 1, 1024, BF16)
        bo_s = view(o_bo, 128, 32, F32)
        bd_s = view(o_bd, 128, 32, F32)
        bu_s = view(o_bu, 128, HC, F32)
        bf_s = view(o_bf, 128, 32, F32)
        eps_s = view(o_eps, 128, 1, F32)
        ones_c = view(o_on1, 128, 1, BF16)
        ones_r = view(o_on2, 1, 128, BF16)
        onesf_r = view(o_on3, 1, 128, F32)
        lau_s = view(o_lau, 128, 32 * 16, BF16)
        lad_s = view(o_lad, 128, HC * 16, BF16)
        st_s = view(o_st, 128, 8, F32)
        wgr = [view(o, 128, 512, BF16) for o in o_wgr]
        rtmp = [view(o, 128, 512, F32) for o in o_rt]
        cev_s = view(o_cev, 128, 1024, BF16)
        fev = [view(o, 128, 512, F32) for o in o_fev]
        qbf = [view(o, 128, 1024, BF16) for o in o_qbf]
        ptil = [view(o, 128, 1024, BF16) for o in o_ptl]
        rrep_s = view(o_rrp, 128, 1024, F32)
        sums_s = view(o_sum, 1, 1024, F32)
        rinv_s = view(o_rin, 1, 1024, F32)
        tl_s = view(o_tl, 16, 512, BF16)
        t2_s = view(o_t2, 16, 512, BF16)
        lbur = [view(o, 16, 512, BF16) for o in o_lbu]
        lbdr = [view(o, 16, 512, BF16) for o in o_lbd]
        xn_a = view(o_xna, 128, 32 * 512, BF16)
        xn_b = view(o_xnb, 128, 32 * 512, BF16)

        xt_s = view(o_ph, 128, D, F32)
        xq_s = view(o_ph + 16384, 128, D, BF16)
        sq_s = view(o_ph + 24576, 128, D, F32)
        dq_off = [o_ph, o_ph + 24576 + 512]
        dqc = [view(dq_off[i], 128, 2048, I32) for i in range(2)]
        dq16 = [view(dq_off[i] + 8192, 128, 2048, F16) for i in range(2)]
        dqa = [view(dq_off[i] + 12288, 128, 2048, F16) for i in range(2)]
        dqt = [view(dq_off[i] + 16384, 128, 2048, F16) for i in range(2)]
        dqw = [view(dq_off[i] + 20480, 128, 2048, BF16) for i in range(2)]
        dqm = [view(dq_off[i] + 24576, 128, 64, F32) for i in range(2)]
        kst = [view(o_ph + i * 12288, 128, NSLOT * 128, BF16) for i in range(2)]
        vst = [view(o_ph + 24576 + i * 12288, 128, NSLOT * 128, BF16) for i in range(2)]
        msk_s = view(o_ph + 49152, 128, 4 * 512, BF16)
        ctv = [view(o_ph + 88064 + i * 1024, 128, 512, BF16) for i in range(4)]
        x2c = [view(o_ph + i * 1024, 128, 512, BF16) for i in range(HC)]
        assert o_ph + 92160 + 8192 <= HEAPB

        psA = ctx.enter_context(nc.psum_tensor("psA", [128, 1024], F32))
        psB = ctx.enter_context(nc.psum_tensor("psB", [128, 1024], F32))
        ps4 = ctx.enter_context(nc.psum_tensor("ps4", [128, 512], F32))
        ps5 = ctx.enter_context(nc.psum_tensor("ps5", [128, 512], F32))
        ps6 = ctx.enter_context(nc.psum_tensor("ps6", [128, 512], F32))
        ps7 = ctx.enter_context(nc.psum_tensor("ps7", [128, 512], F32))
        pbank = [psA.ap()[:, 0:512], psA.ap()[:, 512:1024],
                 psB.ap()[:, 0:512], psB.ap()[:, 512:1024],
                 ps4.ap()[:], ps5.ap()[:], ps6.ap()[:], ps7.ap()[:]]

        R = lambda n: Res(n)
        r_const = R("const")
        r_xt, r_xq, r_st = R("xt"), R("xq"), R("st")
        r_xna, r_xnb = R("xna"), R("xnb")
        r_wgr = [R(f"wgr{i}") for i in range(8)]
        r_rt0, r_rt1 = R("rt0"), R("rt1")
        r_cev = R("cev")
        r_fev = [R(f"fev{i}") for i in range(4)]
        r_qbf = [R(f"qbf{i}") for i in range(4)]
        r_ptl = [R(f"ptl{i}") for i in range(4)]
        r_rrp, r_sumv, r_rin = R("rrp"), R("sumv"), R("rin")
        r_tl, r_t2 = R("tl"), R("t2")
        r_lbur = [R(f"lbur{i}") for i in range(2)]
        r_lbdr = [R(f"lbdr{i}") for i in range(2)]
        r_dq = [R(f"dq{i}") for i in range(2)]
        r_dqc = [R(f"dqc{i}") for i in range(2)]
        r_dqm = [R(f"dqm{i}") for i in range(2)]
        r_kst = [R(f"kst{i}") for i in range(2)]
        r_vst = [R(f"vst{i}") for i in range(2)]
        r_msk = R("msk")
        r_ctv = [R(f"ctv{i}") for i in range(4)]
        r_x2 = [R(f"x2_{i}") for i in range(HC)]
        r_ps = [R(f"ps{i}") for i in range(8)]
        r_psA, r_psB = r_ps[0], r_ps[2]
        r_kown, r_vown, r_kg, r_vg = R("kown"), R("vown"), R("kg"), R("vg")
        r_wuo, r_wdo, r_wug, r_wdg = R("wuo"), R("wdo"), R("wug"), R("wdg")
        r_ctxd, r_atnd = R("ctxd"), R("atnd")

        def bres(i):
            if i < 2:
                return r_ps[0]
            if i < 4:
                return r_ps[2]
            return r_ps[i]

        LDs, LGs, LAs, DVs, ACs, PEs, POs, CCs, OTs = (sem_h[s] for s in ALL_SEMS)
        SH = sem_h

        def wrap(body, emit_w, sem, inc):
            def fn(e):
                for s, c in emit_w.items():
                    e.wait_ge(SH[s], c)
                body(e)
            return fn

        def dstep(eng, reads, writes, mk, n_dma=1):
            sem = ESEM[eng][1]

            def body(e, mk=mk, sem=sem):
                mk(e, lambda inst: inst.then_inc(SH[sem], 16))
            return b.step(eng, sem, 16 * n_dma, reads, writes, body)

        def cstep(eng, reads, writes, mk):
            sem = ESEM[eng][0]

            def body(e, mk=mk, sem=sem):
                last = mk(e)
                last.then_inc(SH[sem], 1)
            return b.step(eng, sem, 1, reads, writes, body)


        class _G:
            pass
        try:
            def ld1(dst, src):
                return dstep("sync", [], [r_const],
                             lambda e, inc, d=dst, s=src: inc(e.dma_start(out=d, in_=s)))
            ld1(cos_s, cosq); ld1(sin_s, sinq); ld1(idn_s, ident_i)
            ld1(btb_s, btbl_i); ld1(kidx_s, kidx_i)
            ld1(bq_s, bq_i); ld1(bk_s, bk_i); ld1(bo_s, bo_i); ld1(bd_s, bd_i)
            ld1(bu_s, bu_i)
            dstep("gpsimd", [], [r_const],
                  lambda e, inc: inc(e.dma_start(out=bv_s, in_=bv_i)))
            dstep("gpsimd", [], [r_const],
                  lambda e, inc: inc(e.dma_start(
                      out=lau_s.rearrange("p (c j) -> p c j", c=32),
                      in_=la_u.rearrange("(c p) j -> p c j", p=128))))
            dstep("gpsimd", [], [r_const],
                  lambda e, inc: inc(e.dma_start(
                      out=lad_s.rearrange("p (c j) -> p c j", c=HC),
                      in_=la_d.rearrange("(c p) j -> p c j", p=128))))
    
            def _ones(v):
                v.memset(ones_c, 1.0)
                v.memset(ones_r, 1.0)
                v.memset(onesf_r, 1.0)
                v.memset(eps_s, EPS)
                return v.tensor_add(out=bf_s, in0=bo_s, in1=bd_s)
            cstep("vector", [r_const], [r_const], _ones)
    
            if KTRUNC < 1:
                raise _Done
            for i in range(8):
                xnr = r_xna if i < 4 else r_xnb
                xnv = xn_a if i < 4 else xn_b
                toff = (i % 4) * 128
                dstep("sync", [], [r_xt],
                      lambda e, inc, i=i: inc(e.dma_start(out=xt_s, in_=x_own[i * 128:(i + 1) * 128, :])))
    
                cstep("vector", [r_xt], [r_st],
                      lambda v: v.reduce_sum(out=st_s[:, 0:1], in_=xt_s, axis=AX.X))
                cstep("gpsimd", [r_st], [r_st],
                      lambda g: g.tensor_scalar(out=st_s[:, 1:2], in0=st_s[:, 0:1],
                                                scalar1=1.0 / D, scalar2=None, op0=OP.mult))
    
                def _sq(v):
                    v.tensor_tensor(out=sq_s, in0=xt_s, in1=xt_s, op=OP.mult)
                    return v.reduce_sum(out=st_s[:, 2:3], in_=sq_s, axis=AX.X)
                cstep("vector", [r_xt, r_st], [r_st], _sq)
    
                cstep("gpsimd", [r_st], [r_st],
                      lambda g: g.tensor_scalar(out=st_s[:, 3:4], in0=st_s[:, 2:3],
                                                scalar1=1.0 / D, scalar2=None, op0=OP.mult))
                cstep("vector", [r_st], [r_st],
                      lambda v: v.scalar_tensor_tensor(out=st_s[:, 4:5], in0=st_s[:, 1:2],
                                                       scalar=st_s[:, 1:2], in1=st_s[:, 3:4],
                                                       op0=OP.mult, op1=OP.subtract))
    
                cstep("scalar", [r_st], [r_st],
                      lambda sc: sc.activation(out=st_s[:, 5:6], in_=st_s[:, 4:5],
                                               func=AF.Ln, bias=eps_s, scale=-1.0))
                cstep("vector", [r_st], [r_st],
                      lambda v: v.tensor_copy(out=st_s[:, 7:8], in_=st_s[:, 5:6]))
                cstep("scalar", [r_st], [r_st],
                      lambda sc: sc.activation(out=st_s[:, 6:7], in_=st_s[:, 7:8],
                                               func=AF.Exp, scale=-0.5))
    
                def _norm(v):
                    v.tensor_scalar(out=sq_s, in0=xt_s, scalar1=st_s[:, 1:2],
                                    scalar2=None, op0=OP.subtract)
                    return v.tensor_scalar(out=xq_s, in0=sq_s, scalar1=st_s[:, 6:7],
                                           scalar2=None, op0=OP.mult)
                cstep("vector", [r_xt, r_st], [r_xq], _norm)
    
                for j in range(32):
                    bank = r_ps[6] if j % 2 == 0 else r_ps[7]
                    pv = (ps6 if j % 2 == 0 else ps7).ap().bitcast(BF16)[:, 0:128]
                    cstep("tensor", [r_xq, r_const], [bank],
                          lambda t, j=j, pv=pv: t.transpose(pv, xq_s[:, j * 128:(j + 1) * 128], idn_s))
                    cstep("vector", [bank], [xnr],
                          lambda v, j=j, pv=pv, xnv=xnv, toff=toff:
                          v.tensor_copy(out=xnv[:, j * 512 + toff: j * 512 + toff + 128], in_=pv))
    
            if KTRUNC < 2:
                raise _Done
            ring = [0]
    
            def next_ring():
                i = ring[0] % 8
                ring[0] += 1
                return i
    
            def xslice(d, th):
                return (xn_a if th == 0 else xn_b)[:, d * 512:(d + 1) * 512]
    
            for gq in range(2):
                grans = []
                for d in range(32):
                    i = next_ring()
                    dstep("gpsimd", [], [r_wgr[i]],
                          lambda e, inc, d=d, gq=gq, i=i: inc(e.dma_start(
                              out=wgr[i], in_=Wk[d * 128:(d + 1) * 128, gq * 512:(gq + 1) * 512])))
                    def _mmk(t, i=i, d=d):
                        last = None
                        for sub in range(4):
                            for th in range(2):
                                last = t.matmul(pbank[sub * 2 + th], wgr[i][:, sub * 128:(sub + 1) * 128],
                                                xslice(d, th), start=(d == 0), stop=(d == 31))
                        return last
                    cstep("tensor", [r_wgr[i], r_xna, r_xnb],
                          [bres(k) for k in range(8)], _mmk)
                if KDBG and gq == 1:
                    cstep("vector", [r_ps[0]], [r_fev[0]],
                          lambda v: v.tensor_copy(out=fev[0], in_=pbank[0]))
                    dstep("sync", [r_fev[0]], [],
                          lambda e, inc: inc(e.dma_start(out=dbg_kps, in_=fev[0])))
                for sub in range(4):
                    g = gq * 4 + sub
                    for th in range(2):
                        bi = sub * 2 + th
                        cs = cos_s[:, th * 512:(th + 1) * 512]
                        ss = sin_s[:, th * 512:(th + 1) * 512]
    
                        cstep("vector", [bres(bi), r_const], [r_rt0],
                              lambda v, bi=bi, g=g: v.tensor_scalar(
                                  out=rtmp[0], in0=pbank[bi], scalar1=bk_s[:, g:g + 1],
                                  scalar2=None, op0=OP.add))
    
                        if "w" not in KSW:
                            def _swp(e, inc):
                                inc(e.dma_start(out=rtmp[1][0:64, :], in_=rtmp[0][64:128, :]))
                                inc(e.dma_start(out=rtmp[1][64:128, :], in_=rtmp[0][0:64, :]))
                            dstep("sync", [r_rt0], [r_rt1], _swp, n_dma=2)

                        if "r" not in KSW:
                            def _ropek(v, cs=cs, ss=ss):
                                v.tensor_tensor(out=rtmp[0], in0=rtmp[0], in1=cs, op=OP.mult)
                                v.tensor_tensor(out=rtmp[1], in0=rtmp[1], in1=ss, op=OP.mult)
                                return v.tensor_add(out=cev_s[:, 0:512], in0=rtmp[0], in1=rtmp[1])
                            cstep("vector", [r_rt0, r_rt1, r_const], [r_rt0, r_rt1, r_cev], _ropek)
                        else:
                            cstep("vector", [r_rt0], [r_cev],
                                  lambda v: v.tensor_copy(out=cev_s[:, 0:512], in_=rtmp[0]))
                        dst = k_own.ap().rearrange("(g bl kd) u -> g kd bl u", g=G, bl=8)[g, :, th * 4:th * 4 + 4, :]
                        src = cev_s[:, 0:512].rearrange("kd (bl u) -> kd bl u", bl=4)
                        if "s" not in KSW:
                            dstep("sync", [r_cev], [r_kown],
                                  lambda e, inc, dst=dst, src=src: inc(
                                      e.dma_start(out=dst, in_=src)))
    
            if KT2 < 1:
                raise _Done
            for vh in range(2):
                grans = []
                for d in range(32):
                    i = next_ring()
                    dstep("gpsimd", [], [r_wgr[i]],
                          lambda e, inc, d=d, vh=vh, i=i: inc(e.dma_start(
                              out=wgr[i], in_=Wv[d * 128:(d + 1) * 128, vh * 512:(vh + 1) * 512])))
                    def _mmv(t, i=i, d=d):
                        last = None
                        for tc in range(8):
                            xnh = xn_a if tc < 4 else xn_b
                            toff = (tc % 4) * 128
                            last = t.matmul(pbank[tc], xnh[:, d * 512 + toff: d * 512 + toff + 128],
                                            wgr[i], start=(d == 0), stop=False)
                        return last
                    cstep("tensor", [r_wgr[i], r_xna, r_xnb],
                          [bres(k) for k in range(8)], _mmv)
    
                def _mvb(t, vh=vh):
                    last = None
                    for tc in range(8):
                        last = t.matmul(pbank[tc], ones_r, bv_s[:, vh * 512:(vh + 1) * 512],
                                        start=False, stop=True)
                    return last
                cstep("tensor", [r_const], [bres(k) for k in range(8)], _mvb)
                for tc in range(8):
                    cstep("vector", [bres(tc)], [r_cev],
                          lambda v, tc=tc: v.tensor_copy(out=cev_s[:, 0:512], in_=pbank[tc]))
                    dst = v_own.ap().rearrange("(g bl u) k -> bl u g k", g=G, bl=8)[tc, :, vh * 4:vh * 4 + 4, :]
                    src = cev_s[:, 0:512].rearrange("u (g k) -> u g k", g=4)
                    dstep("sync", [r_cev], [r_vown],
                          lambda e, inc, dst=dst, src=src: inc(e.dma_start(out=dst, in_=src)))
    
            if KT2 < 2:
                raise _Done
            def _cck(e):
                e.collective_compute("AllGather", OP.bypass,
                                     replica_groups=[[0, 1, 2, 3], [4, 5, 6, 7]],
                                     ins=[k_own.ap().opt()], outs=[k_gth.ap().opt()]).then_inc(SH["CC"], 1)
            b.step("gpsimd", "CC", 1, [r_kown], [r_kg], _cck)
    
            def _ccv(e):
                e.collective_compute("AllGather", OP.bypass,
                                     replica_groups=[[0, 1, 2, 3], [4, 5, 6, 7]],
                                     ins=[v_own.ap().opt()], outs=[v_gth.ap().opt()]).then_inc(SH["CC"], 1)
            b.step("gpsimd", "CC", 1, [r_vown], [r_vg], _ccv)
    
            if KTRUNC < 3:
                raise _Done
            gate_p3 = b.snapshot()
            for rr in (r_dq + r_dqc + r_dqm):
                rr.init = dict(gate_p3)
            units = []
            for hcl in range(UP_PAD):
                for sub2 in range(2):
                    units.append(("u", hcl, sub2))
            for dr in range(4):
                for sub2 in range(6):
                    units.append(("d", dr, sub2))
            for ui, (kind, a, sub2) in enumerate(units):
                en = ui % 2
                eng = "vector" if en == 0 else "gpsimd"
                if kind == "u":
                    w = 2048
                    csrc = cod_u[a * 128:(a + 1) * 128, sub2 * 2048:(sub2 + 1) * 2048]
                    msrc = amx_u[a * 128:(a + 1) * 128, sub2 * 32:(sub2 + 1) * 32]
                    nblk = 32
                    wdst = wu_own.ap()[a * 128:(a + 1) * 128, sub2 * 2048:(sub2 + 1) * 2048]
                    wres = r_wuo
                else:
                    w = 2048 if sub2 < 5 else 768
                    csrc = cod_d[a * 128:(a + 1) * 128, sub2 * 2048: sub2 * 2048 + w]
                    msrc = amx_d[a * 128:(a + 1) * 128, sub2 * 32: sub2 * 32 + w // 64]
                    nblk = w // 64
                    wdst = wd_own.ap()[a * 128:(a + 1) * 128, sub2 * 2048: sub2 * 2048 + w]
                    wres = r_wdo
                dstep("sync", [], [r_dqc[en]],
                      lambda e, inc, en=en, w=w, csrc=csrc: inc(
                          e.dma_start(out=dqc[en][:, 0:w], in_=csrc)))
                dstep("sync", [], [r_dqm[en]],
                      lambda e, inc, en=en, nblk=nblk, msrc=msrc: inc(
                          e.dma_start(out=dqm[en][:, 0:nblk], in_=msrc)))
    
                def _dq(e, en=en, w=w, nblk=nblk):
                    e.tensor_copy(out=dq16[en][:, 0:w], in_=dqc[en][:, 0:w])
                    e.memset(dqa[en][:, 0:w], 0.0)
                    for kk in range(16):
                        e.tensor_scalar(out=dqt[en][:, 0:w], in0=dq16[en][:, 0:w],
                                        scalar1=float(kk), scalar2=float(NF4[kk]),
                                        op0=OP.is_equal, op1=OP.mult)
                        e.tensor_tensor(out=dqa[en][:, 0:w], in0=dqa[en][:, 0:w],
                                        in1=dqt[en][:, 0:w], op=OP.add)
                    return e.tensor_tensor(
                        out=dqw[en][:, 0:w].rearrange("p (a c) -> p a c", c=64),
                        in0=dqa[en][:, 0:w].rearrange("p (a c) -> p a c", c=64),
                        in1=dqm[en][:, 0:nblk].unsqueeze(2).to_broadcast([128, nblk, 64]),
                        op=OP.mult)
                cstep(eng, [r_dqc[en], r_dqm[en]], [r_dq[en]], _dq)
                dstep("scalar", [r_dq[en]], [wres],
                      lambda e, inc, en=en, w=w, wdst=wdst: inc(
                          e.dma_start(out=wdst, in_=dqw[en][:, 0:w])))
    
            def _ccwu(e):
                e.collective_compute("AllGather", OP.bypass,
                                     replica_groups=[list(range(8))],
                                     ins=[wu_own.ap().opt()], outs=[wu_gth.ap().opt()]).then_inc(SH["CC"], 1)
            b.step("gpsimd", "CC", 1, [r_wuo], [r_wug], _ccwu)
    
            def _ccwd(e):
                e.collective_compute("AllGather", OP.bypass,
                                     replica_groups=[list(range(8))],
                                     ins=[wd_own.ap().opt()], outs=[wd_gth.ap().opt()]).then_inc(SH["CC"], 1)
            b.step("gpsimd", "CC", 1, [r_wdo], [r_wdg], _ccwd)
    
            if KTRUNC < 4:
                raise _Done
            gate_p4 = b.snapshot()
            for rr in (r_kst + r_vst + [r_msk] + r_qbf + r_ptl + [r_rrp, r_sumv, r_rin]):
                rr.init = dict(gate_p4)
            dstep("sync", [], [r_msk],
                  lambda e, inc: inc(e.dma_start(out=msk_s, in_=masks_i)))
    
            for g in range(G):
                sb = g % 2
                for s in range(NSLOT):
                    idx = kidx_s[:, g * NSLOT + s: g * NSLOT + s + 1]
    
                    def _stg(e, inc, s=s, sb=sb, idx=idx):
                        inc(e.indirect_dma_start(kst[sb][:, s * 128:(s + 1) * 128], None,
                                                 k_gth.ap(), IndirectOffsetOnAxis(idx, 0)))
                        inc(e.indirect_dma_start(vst[sb][:, s * 128:(s + 1) * 128], None,
                                                 v_gth.ap(), IndirectOffsetOnAxis(idx, 0)))
                    dstep("gpsimd", [r_kg, r_vg, r_const], [r_kst[sb], r_vst[sb]], _stg, n_dma=2)
    
                grans = []
                for d in range(32):
                    i = next_ring()
                    dstep("gpsimd", [], [r_wgr[i]],
                          lambda e, inc, d=d, g=g, i=i: inc(e.dma_start(
                              out=wgr[i], in_=Wq[d * 128:(d + 1) * 128, g * 512:(g + 1) * 512])))
                    def _mmq(t, i=i, d=d):
                        last = None
                        for rh in range(4):
                            for th in range(2):
                                last = t.matmul(pbank[rh * 2 + th], wgr[i][:, rh * 128:(rh + 1) * 128],
                                                xslice(d, th), start=(d == 0), stop=(d == 31))
                        return last
                    cstep("tensor", [r_wgr[i], r_xna, r_xnb],
                          [bres(k) for k in range(8)], _mmq)
                for rh in range(4):
                    h = g * 4 + rh
                    for th in range(2):
                        bi = rh * 2 + th
                        cs = cos_s[:, th * 512:(th + 1) * 512]
                        ss = sin_s[:, th * 512:(th + 1) * 512]
    
                        cstep("vector", [bres(bi), r_const], [r_rt0],
                              lambda v, bi=bi, h=h: v.tensor_scalar(
                                  out=rtmp[0], in0=pbank[bi], scalar1=bq_s[:, h:h + 1],
                                  scalar2=None, op0=OP.add))
    
                        def _swpq(e, inc):
                            inc(e.dma_start(out=rtmp[1][0:64, :], in_=rtmp[0][64:128, :]))
                            inc(e.dma_start(out=rtmp[1][64:128, :], in_=rtmp[0][0:64, :]))
                        dstep("sync", [r_rt0], [r_rt1], _swpq, n_dma=2)
    
                        def _ropeq(v, cs=cs, ss=ss, rh=rh, th=th):
                            v.tensor_tensor(out=rtmp[0], in0=rtmp[0], in1=cs, op=OP.mult)
                            v.tensor_tensor(out=rtmp[1], in0=rtmp[1], in1=ss, op=OP.mult)
                            return v.tensor_add(out=qbf[rh][:, th * 512:(th + 1) * 512],
                                                in0=rtmp[0], in1=rtmp[1])
                        cstep("vector", [r_rt0, r_rt1, r_const], [r_rt0, r_rt1, r_qbf[rh]], _ropeq)
    
                for rh in range(4):
                    h = g * 4 + rh
                    for ch in range(2):
                        base = 0 if ch == 0 else NSL_LO
                        npair = (NSL_LO if ch == 0 else NSL_HI) // 2
                        ctx_r = r_ps[4] if ch == 0 else r_ps[5]
                        ctx_b = ps4 if ch == 0 else ps5
                        for pr in range(npair):
                            s0 = base + 2 * pr
                            sc_r = r_psA if pr % 2 == 0 else r_psB
                            sc_b = psA if pr % 2 == 0 else psB
    
                            def _mm1(t, s0=s0, sc_b=sc_b, rh=rh, ch=ch, sb=sb):
                                last = None
                                for k in range(2):
                                    last = t.matmul(sc_b.ap()[:, k * 512:(k + 1) * 512],
                                                    kst[sb][:, (s0 + k) * 128:(s0 + k + 1) * 128],
                                                    qbf[rh][:, ch * 512:(ch + 1) * 512],
                                                    start=True, stop=True)
                                return last
                            cstep("tensor", [r_kst[sb], r_qbf[rh]], [sc_r], _mm1)
    
                            pi = (ch * 16 + pr) % 4
    
                            def _exp(sc, sc_b=sc_b, pi=pi, s0=s0):
                                return sc.activation(out=ptil[pi], in_=sc_b.ap(),
                                                     func=AF.Exp, bias=btb_s[:, s0:s0 + 1],
                                                     scale=SCALE)
                            cstep("scalar", [sc_r, r_const], [r_ptl[pi]], _exp)
    
                            if pr < 2:
                                def _msk2(v, pi=pi, pr=pr):
                                    v.tensor_tensor(out=ptil[pi][:, 0:512],
                                                    in0=ptil[pi][:, 0:512],
                                                    in1=msk_s[:, 2 * pr * 512:(2 * pr + 1) * 512],
                                                    op=OP.mult)
                                    return v.tensor_tensor(out=ptil[pi][:, 512:1024],
                                                           in0=ptil[pi][:, 512:1024],
                                                           in1=msk_s[:, (2 * pr + 1) * 512:(2 * pr + 2) * 512],
                                                           op=OP.mult)
                                cstep("vector", [r_msk], [r_ptl[pi]], _msk2)
    
                            def _mm2(t, pi=pi, s0=s0, pr=pr, npair=npair, ctx_b=ctx_b, sb=sb):
                                for k in range(2):
                                    t.matmul(ps6.ap()[0:1, 0:512], ones_c,
                                             ptil[pi][:, k * 512:(k + 1) * 512],
                                             start=(pr == 0 and k == 0),
                                             stop=(pr == npair - 1 and k == 1))
                                last = None
                                for k in range(2):
                                    last = t.matmul(ctx_b.ap(),
                                                    vst[sb][:, (s0 + k) * 128:(s0 + k + 1) * 128],
                                                    ptil[pi][:, k * 512:(k + 1) * 512],
                                                    start=(pr == 0 and k == 0),
                                                    stop=(pr == npair - 1 and k == 1))
                                return last
                            cstep("tensor", [r_ptl[pi], r_vst[sb], r_const],
                                  [r_ps[6], ctx_r], _mm2)
                        cstep("scalar", [r_ps[6]], [r_sumv],
                              lambda sc, ch=ch: sc.copy(out=sums_s[:, ch * 512:(ch + 1) * 512],
                                                        in_=ps6.ap()[0:1, 0:512]))
                        cstep("vector", [r_sumv], [r_rin],
                              lambda v, ch=ch: v.reciprocal(out=rinv_s[:, ch * 512:(ch + 1) * 512],
                                                            in_=sums_s[:, ch * 512:(ch + 1) * 512]))
                    for ch in range(2):
                        cstep("tensor", [r_rin, r_const], [r_ps[7]],
                              lambda t, ch=ch: t.matmul(ps7.ap(), onesf_r,
                                                        rinv_s[:, ch * 512:(ch + 1) * 512],
                                                        start=True, stop=True))
                        cstep("vector", [r_ps[7]], [r_rrp],
                              lambda v, ch=ch: v.tensor_copy(out=rrep_s[:, ch * 512:(ch + 1) * 512],
                                                             in_=ps7.ap()))
    
                    def _cev2(v):
                        v.tensor_tensor(out=cev_s[:, 0:512], in0=ps4.ap(),
                                        in1=rrep_s[:, 0:512], op=OP.mult)
                        return v.tensor_tensor(out=cev_s[:, 512:1024], in0=ps5.ap(),
                                               in1=rrep_s[:, 512:1024], op=OP.mult)
                    cstep("vector", [r_ps[4], r_ps[5], r_rrp], [r_cev], _cev2)
                    dstep("sync", [r_cev], [r_ctxd],
                          lambda e, inc, h=h: inc(e.dma_start(
                              out=ctx_dr.ap()[h * 128:(h + 1) * 128, :], in_=cev_s)))
    
            if KTRUNC < 5:
                raise _Done
            gate_p5 = b.snapshot()
            for rr in r_ctv:
                rr.init = dict(gate_p5)
            for dq_ in range(8):
                grans = []
                for cd in range(32):
                    i = next_ring()
                    dstep("gpsimd", [], [r_wgr[i]],
                          lambda e, inc, cd=cd, dq_=dq_, i=i: inc(e.dma_start(
                              out=wgr[i], in_=Wo[cd * 128:(cd + 1) * 128, dq_ * 512:(dq_ + 1) * 512])))
                    ci = cd % 2
    
                    def _ctl(e, inc, cd=cd, ci=ci):
                        inc(e.dma_start(out=ctv[2 * ci], in_=ctx_dr.ap()[cd * 128:(cd + 1) * 128, 0:512]))
                        inc(e.dma_start(out=ctv[2 * ci + 1], in_=ctx_dr.ap()[cd * 128:(cd + 1) * 128, 512:1024]))
                    dstep("sync", [r_ctxd], [r_ctv[2 * ci], r_ctv[2 * ci + 1]], _ctl, n_dma=2)

                    def _mmo(t, i=i, cd=cd, ci=ci):
                        last = None
                        for sub in range(4):
                            for th in range(2):
                                last = t.matmul(pbank[sub * 2 + th], wgr[i][:, sub * 128:(sub + 1) * 128],
                                                ctv[2 * ci + th], start=(cd == 0), stop=(cd == 31))
                        return last
                    cstep("tensor", [r_wgr[i], r_ctv[2 * ci], r_ctv[2 * ci + 1]],
                          [bres(k) for k in range(8)], _mmo)
                for sub in range(4):
                    dout = dq_ * 4 + sub
                    for th in range(2):
                        fi = 2 * (sub % 2) + th
                        cstep("vector", [bres(sub * 2 + th)], [r_fev[fi]],
                              lambda v, sub=sub, th=th, fi=fi: v.tensor_copy(
                                  out=fev[fi], in_=pbank[sub * 2 + th]))
                        dstep("sync", [r_fev[fi]], [r_atnd],
                              lambda e, inc, dout=dout, th=th, fi=fi: inc(e.dma_start(
                                  out=atn_dr.ap()[dout * 128:(dout + 1) * 128,
                                                  th * 512:(th + 1) * 512], in_=fev[fi])))
    
            if KTRUNC < 6:
                raise _Done
            gate_p6 = b.snapshot()
            for rr in r_x2:
                rr.init = dict(gate_p6)
            out_sem_total = [0]
            for th in range(2):
                def _tlmm(t, th=th):
                    last = None
                    for d in range(32):
                        last = t.matmul(ps6.ap()[0:16, 0:512], lau_s[:, d * 16:(d + 1) * 16],
                                        xslice(d, th), start=(d == 0), stop=(d == 31))
                    return last
                cstep("tensor", [r_const, r_xna, r_xnb], [r_ps[6]], _tlmm)
                cstep("vector", [r_ps[6]], [r_tl],
                      lambda v: v.tensor_copy(out=tl_s, in_=ps6.ap()[0:16, 0:512]))
    
                if KT6 < 1:
                    raise _Done
                hquads = []
                for r in range(8):
                    qsz = [4, 4, UP_N[r] - 8]
                    base = 0
                    for qs in qsz:
                        hquads.append((r, base, qs))
                        base += qs
                brot = [0]
                for (r, hbase, qs) in hquads:
                    li = (hbase // 4 + r) % 2
                    hg0 = up_glob(r, hbase)
                    dstep("gpsimd", [], [r_lbur[li]],
                          lambda e, inc, hg0=hg0, qs=qs, li=li: inc(e.dma_start(
                              out=lbur[li][:, 0:qs * 128],
                              in_=lb_u[:, hg0 * 128:(hg0 + qs) * 128])))
                    banks = [(brot[0] + j) % 8 for j in range(qs)]
                    for d in range(32):
                        i = next_ring()
                        src = wu_gth.ap()[r * UP_PAD * 128 + hbase * 128:
                                          r * UP_PAD * 128 + (hbase + qs) * 128,
                                          d * 128:(d + 1) * 128]
                        dstep("sync", [r_wug], [r_wgr[i]],
                              lambda e, inc, src=src, i=i, qs=qs: inc(
                                  e.dma_start_transpose(wgr[i][:, 0:qs * 128], src)))

                        def _mmu(t, i=i, d=d, qs=qs, banks=banks, th=th):
                            last = None
                            for j in range(qs):
                                last = t.matmul(pbank[banks[j]], wgr[i][:, j * 128:(j + 1) * 128],
                                                xslice(d, th), start=(d == 0), stop=False)
                            return last
                        cstep("tensor", [r_wgr[i], r_xna, r_xnb],
                              list({bres(x) for x in banks}), _mmu)
    
                    def _mlu(t, qs=qs, banks=banks, li=li):
                        last = None
                        for j in range(qs):
                            last = t.matmul(pbank[banks[j]], lbur[li][:, j * 128:(j + 1) * 128],
                                            tl_s, start=False, stop=True)
                        return last
                    cstep("tensor", [r_tl, r_lbur[li]], list({bres(x) for x in banks}), _mlu)
                    for j in range(qs):
                        hg = up_glob(r, hbase + j)
                        cstep("scalar", [bres(banks[j]), r_const], [r_x2[hg]],
                              lambda sc, hg=hg, bj=banks[j]: sc.activation(
                                  out=x2c[hg], in_=pbank[bj], func=AF.Relu,
                                  bias=bu_s[:, hg:hg + 1]))
                    brot[0] = (brot[0] + qs) % 8
    
                if KT6 < 2:
                    raise _Done
                def _t2mm(t):
                    last = None
                    for hc_ in range(HC):
                        last = t.matmul(ps6.ap()[0:16, 0:512], lad_s[:, hc_ * 16:(hc_ + 1) * 16],
                                        x2c[hc_], start=(hc_ == 0), stop=(hc_ == HC - 1))
                    return last
                cstep("tensor", [r_const] + r_x2, [r_ps[6]], _t2mm)
                cstep("vector", [r_ps[6]], [r_t2],
                      lambda v: v.tensor_copy(out=t2_s, in_=ps6.ap()[0:16, 0:512]))
    
                if KT6 < 3:
                    raise _Done
                for r in range(8):
                    banks = [(4 * (r % 2)) + j for j in range(4)]
                    li = r % 2
                    dstep("gpsimd", [], [r_lbdr[li]],
                          lambda e, inc, r=r, li=li: inc(e.dma_start(
                              out=lbdr[li][:, 0:512], in_=lb_d[:, r * 512:(r + 1) * 512])))
                    for hc_ in range(HC):
                        i = next_ring()
                        src = wd_gth.ap()[r * 512:(r + 1) * 512, hc_ * 128:(hc_ + 1) * 128]
                        dstep("scalar", [r_wdg], [r_wgr[i]],
                              lambda e, inc, src=src, i=i: inc(e.dma_start_transpose(wgr[i], src)))

                        def _mmd(t, i=i, hc_=hc_, banks=banks):
                            last = None
                            for j in range(4):
                                last = t.matmul(pbank[banks[j]], wgr[i][:, j * 128:(j + 1) * 128],
                                                x2c[hc_], start=(hc_ == 0), stop=False)
                            return last
                        cstep("tensor", [r_wgr[i], r_x2[hc_]],
                              list({bres(x) for x in banks}), _mmd)
    
                    def _mld(t, r=r, banks=banks, li=li):
                        last = None
                        for j in range(4):
                            last = t.matmul(pbank[banks[j]], lbdr[li][:, j * 128:(j + 1) * 128],
                                            t2_s, start=False, stop=True)
                        return last
                    cstep("tensor", [r_t2, r_lbdr[li]], list({bres(x) for x in banks}), _mld)
                    for j in range(4):
                        if "f" in KD:
                            continue
                        dout = r * 4 + j
                        fa = 2 * (j % 2)
    
                        def _fl(e, inc, dout=dout, th=th, fa=fa):
                            inc(e.dma_start(out=fev[fa],
                                            in_=atn_dr.ap()[dout * 128:(dout + 1) * 128,
                                                            th * 512:(th + 1) * 512]))
                            inc(e.dma_start(out=fev[fa + 1],
                                            in_=x_ownT[dout * 128:(dout + 1) * 128,
                                                       th * 512:(th + 1) * 512]))
                        dstep("sync", [r_atnd], [r_fev[fa], r_fev[fa + 1]], _fl, n_dma=2)
    
                        if "v" not in KD:
                            def _fin(v, dout=dout, fa=fa, bj=banks[j]):
                                v.scalar_tensor_tensor(out=fev[fa], in0=pbank[bj],
                                                       scalar=bf_s[:, dout:dout + 1],
                                                       in1=fev[fa], op0=OP.add, op1=OP.add)
                                return v.tensor_add(out=fev[fa], in0=fev[fa], in1=fev[fa + 1])
                            cstep("vector", [bres(banks[j]), r_fev[fa + 1], r_const],
                                  [r_fev[fa]], _fin)
                        else:
                            cstep("vector", [bres(banks[j]), r_fev[fa + 1], r_const],
                                  [r_fev[fa]],
                                  lambda v, fa=fa: v.tensor_copy(out=fev[fa], in_=fev[fa + 1]))
    
                        def _fst(e, inc, dout=dout, th=th, fa=fa):
                            inc(e.dma_start(out=out_fm[dout * 128:(dout + 1) * 128,
                                                       th * 512:(th + 1) * 512],
                                            in_=fev[fa]))
                        b.step("sync", "OT", 16, [r_fev[fa]], [],
                               lambda e, fa=fa, dout=dout, th=th: e.dma_start(
                                   out=out_fm[th * D + dout * 128: th * D + (dout + 1) * 128, :],
                                   in_=fev[fa]).then_inc(SH["OT"], 16))
                        out_sem_total[0] += 16
    
            if KDBG:
                def _dbg(e, inc):
                    inc(e.dma_start(out=dbg_kg, in_=k_gth.ap()))
                    inc(e.dma_start(out=dbg_vg, in_=v_gth.ap()))
                    inc(e.dma_start(out=dbg_ctx, in_=ctx_dr.ap()))
                    inc(e.dma_start(out=dbg_atn, in_=atn_dr.ap()))
                    inc(e.dma_start(out=dbg_wu, in_=wu_own.ap()))
                dstep("sync", [r_kg, r_vg, r_ctxd, r_atnd, r_wuo], [], _dbg, n_dma=5)
                dstep("sync", [r_xna], [],
                      lambda e, inc: inc(e.dma_start(out=dbg_xn, in_=xn_a)))
                dstep("sync", [r_st], [],
                      lambda e, inc: inc(e.dma_start(out=dbg_st, in_=st_s)))
                def _dbg2(e, inc):
                    for hc_ in range(HC):
                        inc(e.dma_start(out=dbg_x2[:, hc_ * 512:(hc_ + 1) * 512], in_=x2c[hc_]))
                    inc(e.dma_start(out=dbg_tl[:, 0:512], in_=tl_s))
                    inc(e.dma_start(out=dbg_tl[:, 512:1024], in_=t2_s))
                dstep("sync", r_x2 + [r_tl, r_t2], [], _dbg2, n_dma=HC + 2)
            v_out = out_sem_total[0]
            b.step("sync", "LD", 0, [], [],
                   lambda e: e.wait_ge(SH["OT"], v_out))
        except _Done:
            pass

        with nc.Block() as block:
            def replay(e, eng):
                for (emit_w, body, semn, inc) in b.q[eng]:
                    for s, c in emit_w.items():
                        e.wait_ge(SH[s], c)
                    body(e)

            @block.sync
            def _(e):
                replay(e, "sync")

            @block.gpsimd
            def _(e):
                replay(e, "gpsimd")

            @block.vector
            def _(e):
                replay(e, "vector")

            @block.scalar
            def _(e):
                replay(e, "scalar")

            @block.tensor
            def _(e):
                replay(e, "tensor")

    return nc


def _core_tables(p):
    A = list(range(4 * p, 4 * p + 4))
    Bb = list(range(28 - 4 * p, 32 - 4 * p))
    own = A + Bb

    def owner_bl(ub):
        if ub < 16:
            r = ub // 4
            bl = ub - 4 * r
        else:
            r = 7 - ub // 4
            bl = ub - (28 - 4 * r) + 4
        return r, bl

    ab_lo = 4 * p
    lo = [ab_lo + k for k in range(4)] + list(range(0, ab_lo)) + [None] * (12 - ab_lo)
    ab_hi = 28 - 4 * p
    hi = [ab_hi + k for k in range(4)] + list(range(0, ab_hi)) + [None] * (4 * p)
    slots = lo + hi
    assert len(lo) == NSL_LO and len(hi) == NSL_HI

    kidx = np.zeros((128, G * NSLOT), np.int32)
    bias = np.zeros((128, NSLOT), np.float32)
    u = np.arange(128)
    for s, ub in enumerate(slots):
        if ub is None:
            row = 0
            bias[:, s] = NEGB
        else:
            r, bl = owner_bl(ub)
            row = (r * G * 8 + bl) * 128
            bias[:, s] = -SHIFT
        for g in range(G):
            base = row + (0 if ub is None else g * 8 * 128)
            kidx[:, g * NSLOT + s] = base + u
    return own, kidx, bias


def kernel(L0_attn_hidden_states, L0_attn_Wq, L0_attn_bq, L0_attn_Wk, L0_attn_bk,
           L0_attn_Wv, L0_attn_bv, L0_attn_Wo, L0_attn_bo, L0_ffn_w_up_codes,
           L0_ffn_w_up_absmax, L0_ffn_b_up, L0_ffn_w_up_lora_a, L0_ffn_w_up_lora_b,
           L0_ffn_w_down_codes, L0_ffn_w_down_absmax, L0_ffn_b_down,
           L0_ffn_w_down_lora_a, L0_ffn_w_down_lora_b):
    x = np.ascontiguousarray(L0_attn_hidden_states)
    half = 64
    inv = 1.0 / 10000.0 ** (np.arange(half) / half)
    masks = np.zeros((128, 4, 512), np.float32)
    uu = np.arange(128)[:, None]
    jj = np.arange(512)[None, :]
    for k in range(4):
        masks[:, k, :] = (jj >= 128 * k + uu).astype(np.float32)
    masks = masks.reshape(128, 4 * 512).astype(NPBF)
    ident = np.eye(128, dtype=NPBF)

    cod_u_full = L0_ffn_w_up_codes.reshape(HID, D)
    amx_u_full = L0_ffn_w_up_absmax.reshape(HID, 64)
    cod_d_full = L0_ffn_w_down_codes.reshape(D, HID)
    amx_d_full = L0_ffn_w_down_absmax.reshape(D, 172)

    in_maps = []
    meta = []
    for c in range(8):
        bb, p = c // 4, c % 4
        own, kidx, bias = _core_tables(p)
        rows = []
        for ub in own:
            rows.extend(range(ub * 128, ub * 128 + 128))
        rows = np.array(rows)
        x_own = np.ascontiguousarray(x[bb, rows, :]).astype(np.float32)
        x_ownT = np.ascontiguousarray(x_own.T)
        ang = rows.astype(np.float64)[None, :] * inv[:, None]
        cos1 = np.cos(ang)
        sin1 = np.sin(ang)
        cosq = np.concatenate([cos1, cos1], 0).astype(NPBF)
        sinq = np.concatenate([-sin1, sin1], 0).astype(NPBF)

        g0 = sum(UP_N[:c])
        n = UP_N[c]
        cu = np.zeros((UP_PAD * 128, D), np.int32)
        au = np.zeros((UP_PAD * 128, 64), np.float32)
        cu[:n * 128] = cod_u_full[g0 * 128:(g0 + n) * 128]
        au[:n * 128] = amx_u_full[g0 * 128:(g0 + n) * 128]

        im = {
            "x_own": x_own, "x_ownT": x_ownT,
            "Wq": L0_attn_Wq, "Wk": L0_attn_Wk, "Wv": L0_attn_Wv, "Wo": L0_attn_Wo,
            "bq_i": np.ascontiguousarray(L0_attn_bq.reshape(32, 128).T),
            "bk_i": np.ascontiguousarray(L0_attn_bk.reshape(8, 128).T),
            "bv_i": L0_attn_bv.reshape(1, 1024),
            "bo_i": np.ascontiguousarray(L0_attn_bo.reshape(32, 128).T),
            "bd_i": np.ascontiguousarray(L0_ffn_b_down.reshape(32, 128).T),
            "bu_i": np.ascontiguousarray(L0_ffn_b_up.reshape(HC, 128).T),
            "cosq": cosq, "sinq": sinq, "masks_i": masks, "ident_i": ident,
            "btbl_i": bias, "kidx_i": kidx,
            "cod_u": cu, "amx_u": au,
            "cod_d": np.ascontiguousarray(cod_d_full[c * 512:(c + 1) * 512]),
            "amx_d": np.ascontiguousarray(amx_d_full[c * 512:(c + 1) * 512]),
            "la_u": L0_ffn_w_up_lora_a, "lb_u": L0_ffn_w_up_lora_b,
            "la_d": L0_ffn_w_down_lora_a, "lb_d": L0_ffn_w_down_lora_b,
        }
        in_maps.append(im)
        meta.append((bb, rows))

    nc = build_nc()
    res = run_bass_kernel_spmd(nc, in_maps, core_ids=list(range(8)))

    out = np.zeros((BB, S, D), np.float32)
    for c in range(8):
        bb, rows = meta[c]
        shard = res.results[c]["out_fm"]
        out[bb, rows[0:512], :] = shard[0:D, :].T
        out[bb, rows[512:1024], :] = shard[D:2 * D, :].T
    return out


# version 05
# speedup vs baseline: 1.7363x; 1.7363x over previous
import math
import os
import numpy as np
import ml_dtypes
from contextlib import ExitStack
import concourse.bass as bass
import concourse.mybir as mybir
from concourse.bass_core_rust import IndirectOffsetOnAxis
from concourse.bass_utils import run_bass_kernel_spmd

F32 = mybir.dt.float32
F16 = mybir.dt.float16
BF16 = mybir.dt.bfloat16
I32 = mybir.dt.int32
AF = mybir.ActivationFunctionType
OP = mybir.AluOpType
AX = mybir.AxisListType
NPBF = ml_dtypes.bfloat16

BB, S, D = 2, 4096, 4096
H, G, DH = 32, 8, 128
HID = 11008
HC = 86
NSL_LO, NSL_HI = 16, 32
NSLOT = 48
SCALE = 1.0 / math.sqrt(128.0)
SHIFT = 12.0
NEGB = -1e30
EPS = 1e-5
UP_N = [11, 11, 11, 11, 11, 11, 10, 10]
UP_PAD = 11
NF4 = [-1.0, -0.6961928009986877, -0.5250730514526367, -0.39491748809814453,
       -0.28444138169288635, -0.18477343022823334, -0.09105003625154495, 0.0,
       0.07958029955625534, 0.16093020141124725, 0.24611230194568634,
       0.33791524171829224, 0.44070982933044434, 0.5626170039176941,
       0.7229568362236023, 1.0]


def up_glob(r, local):
    return sum(UP_N[:r]) + local


ENGS = ("sync", "gpsimd", "vector", "scalar", "tensor")
ESEM = {"sync": (None, "LD"), "gpsimd": ("PO", "LG"), "vector": ("DV", None),
        "scalar": ("AC", "LA"), "tensor": ("PE", None)}
ALL_SEMS = ["LD", "LG", "LA", "DV", "AC", "PE", "PO", "CC", "OT"]
COMPUTE_SEMS = {"DV": "vector", "AC": "scalar", "PE": "tensor", "PO": "gpsimd"}


class Res:
    __slots__ = ("name", "prod", "readers", "init")

    def __init__(self, name, init=None):
        self.name = name
        self.prod = None
        self.readers = {}
        self.init = dict(init) if init else {}


class Bld:
    def __init__(self):
        self.q = {e: [] for e in ENGS}
        self.ctr = {s: 0 for s in ALL_SEMS}
        self.hw = {e: {s: 0 for s in ALL_SEMS} for e in ENGS}

    def step(self, eng, sem, inc, reads, writes, body):
        waits = {}

        def merge(tok):
            if tok is None:
                return
            s, c = tok
            if c > waits.get(s, 0):
                waits[s] = c

        for r in reads:
            merge(r.prod)
            for s, c in r.init.items():
                merge((s, c))
        for r in writes:
            merge(r.prod)
            for s, c in r.readers.items():
                merge((s, c))
            for s, c in r.init.items():
                merge((s, c))
        emit_w = {}
        for s, c in waits.items():
            if COMPUTE_SEMS.get(s) == eng:
                continue
            if c <= self.hw[eng][s]:
                continue
            emit_w[s] = c
            self.hw[eng][s] = c
        self.ctr[sem] += inc
        cnt = self.ctr[sem]
        self.q[eng].append((emit_w, body, sem, inc))
        tok = (sem, cnt)
        for r in reads:
            if cnt > r.readers.get(sem, 0):
                r.readers[sem] = cnt
        for r in writes:
            r.prod = tok
            r.readers = {}
            r.init = {}
        return tok

    def snapshot(self):
        return dict(self.ctr)


KTRUNC = int(os.environ.get("KTRUNC", "99"))
KT2 = int(os.environ.get("KT2", "99"))
KSW = os.environ.get("KSW", "")
KT6 = int(os.environ.get("KT6", "99"))
KD = os.environ.get("KD", "")
KDBG = os.environ.get("KDBG", "") == "1"


class _Done(Exception):
    pass


def build_nc():
    nc = bass.Bass("TRN2", target_bir_lowering=False)
    dt = nc.dram_tensor

    x_own = dt("x_own", [1024, D], F32, kind="ExternalInput").ap()
    x_ownT = dt("x_ownT", [D, 1024], F32, kind="ExternalInput").ap()
    Wq = dt("Wq", [D, D], F32, kind="ExternalInput").ap()
    Wk = dt("Wk", [D, 1024], F32, kind="ExternalInput").ap()
    Wv = dt("Wv", [D, 1024], F32, kind="ExternalInput").ap()
    Wo = dt("Wo", [D, D], F32, kind="ExternalInput").ap()
    bq_i = dt("bq_i", [128, 32], F32, kind="ExternalInput").ap()
    bk_i = dt("bk_i", [128, 8], F32, kind="ExternalInput").ap()
    bv_i = dt("bv_i", [1, 1024], F32, kind="ExternalInput").ap()
    bo_i = dt("bo_i", [128, 32], F32, kind="ExternalInput").ap()
    bd_i = dt("bd_i", [128, 32], F32, kind="ExternalInput").ap()
    bu_i = dt("bu_i", [128, HC], F32, kind="ExternalInput").ap()
    cosq = dt("cosq", [128, 1024], BF16, kind="ExternalInput").ap()
    sinq = dt("sinq", [128, 1024], BF16, kind="ExternalInput").ap()
    masks_i = dt("masks_i", [128, 4 * 512], BF16, kind="ExternalInput").ap()
    ident_i = dt("ident_i", [128, 128], BF16, kind="ExternalInput").ap()
    btbl_i = dt("btbl_i", [128, NSLOT], F32, kind="ExternalInput").ap()
    kidx_i = dt("kidx_i", [128, G * NSLOT], I32, kind="ExternalInput").ap()
    cod_u = dt("cod_u", [UP_PAD * 128, D], I32, kind="ExternalInput").ap()
    amx_u = dt("amx_u", [UP_PAD * 128, 64], F32, kind="ExternalInput").ap()
    cod_d = dt("cod_d", [512, HID], I32, kind="ExternalInput").ap()
    amx_d = dt("amx_d", [512, 172], F32, kind="ExternalInput").ap()
    la_u = dt("la_u", [D, 16], F32, kind="ExternalInput").ap()
    lb_u = dt("lb_u", [16, HID], F32, kind="ExternalInput").ap()
    la_d = dt("la_d", [HID, 16], F32, kind="ExternalInput").ap()
    lb_d = dt("lb_d", [16, D], F32, kind="ExternalInput").ap()
    out_fm = dt("out_fm", [2 * D, 512], F32, kind="ExternalOutput").ap()

    if KDBG:
        dbg_kg = dt("dbg_kg", [4 * G * 8 * 128, 128], BF16, kind="ExternalOutput").ap()
        dbg_vg = dt("dbg_vg", [4 * G * 8 * 128, 128], BF16, kind="ExternalOutput").ap()
        dbg_ctx = dt("dbg_ctx", [H * 128, 1024], BF16, kind="ExternalOutput").ap()
        dbg_atn = dt("dbg_atn", [D, 1024], F32, kind="ExternalOutput").ap()
        dbg_wu = dt("dbg_wu", [UP_PAD * 128, D], BF16, kind="ExternalOutput").ap()
        dbg_xn = dt("dbg_xn", [128, 32 * 512], BF16, kind="ExternalOutput").ap()
        dbg_x2 = dt("dbg_x2", [128, HC * 512], BF16, kind="ExternalOutput").ap()
        dbg_tl = dt("dbg_tl", [16, 1024], BF16, kind="ExternalOutput").ap()
        dbg_st = dt("dbg_st", [128, 8], F32, kind="ExternalOutput").ap()
        dbg_kps = dt("dbg_kps", [128, 512], F32, kind="ExternalOutput").ap()
    k_own = dt("k_own", [G * 8 * 128, 128], BF16)
    v_own = dt("v_own", [G * 8 * 128, 128], BF16)
    k_gth = dt("k_gth", [4 * G * 8 * 128, 128], BF16)
    v_gth = dt("v_gth", [4 * G * 8 * 128, 128], BF16)
    wu_own = dt("wu_own", [UP_PAD * 128, D], BF16)
    wd_own = dt("wd_own", [512, HID], BF16)
    wu_gth = dt("wu_gth", [8 * UP_PAD * 128, D], BF16)
    wd_gth = dt("wd_gth", [D, HID], BF16)
    ctx_dr = dt("ctx_dr", [H * 128, 1024], BF16)
    atn_dr = dt("atn_dr", [D, 1024], F32)

    b = Bld()

    with ExitStack() as ctx:
        sem_h = {s: ctx.enter_context(nc.semaphore(s)) for s in ALL_SEMS}
        HEAPB = 206 * 1024
        heap_t = ctx.enter_context(nc.sbuf_tensor("heap", [128, HEAPB // 2], BF16))
        heap = heap_t[:]
        cur = [0]

        def carve(nbytes, align=128):
            o = (cur[0] + align - 1) // align * align
            cur[0] = o + nbytes
            assert cur[0] <= HEAPB, f"heap overflow at {cur[0]}"
            return o

        def view(off, parts, n, dtyp):
            sz = {F32: 4, F16: 2, BF16: 2, I32: 4}[dtyp]
            v = heap[0:parts, off // 2: (off + n * sz) // 2]
            if dtyp != BF16:
                v = v.bitcast(dtyp)
            return v

        o_cos = carve(2048); o_sin = carve(2048)
        o_idn = carve(256); o_btb = carve(NSLOT * 4)
        o_kidx = carve(G * NSLOT * 4)
        o_bq = carve(128); o_bk = carve(32); o_bv = carve(2048)
        o_bo = carve(128); o_bd = carve(128); o_bu = carve(HC * 4)
        o_bf = carve(128)
        o_on1 = carve(4); o_on2 = carve(256); o_on3 = carve(512)
        o_eps = carve(128)
        o_lau = carve(32 * 16 * 2); o_lad = carve(HC * 16 * 2)
        o_st = carve(32)
        o_wgr = [carve(1024) for _ in range(8)]
        o_tl = carve(1024); o_t2 = carve(1024)
        o_lbu = [carve(1024) for _ in range(2)]
        o_lbd = [carve(1024) for _ in range(2)]
        o_xna = carve(32 * 1024)
        o_xnb = carve(32 * 1024)
        o_ph = carve(0)
        PH_SZ = HEAPB - o_ph
        assert PH_SZ >= 101000, f"phase area only {PH_SZ}"
        o_rt = [o_ph + 80 * 1024, o_ph + 82 * 1024]
        o_cev = o_ph + 84 * 1024
        o_qbf = [o_ph + 52 * 1024 + i * 2048 for i in range(4)]
        o_ptl = [o_ph + 60 * 1024 + i * 2048 for i in range(4)]
        o_rrp = o_ph + 68 * 1024
        o_sum = o_ph + 72 * 1024
        o_rin = o_ph + 76 * 1024
        o_fev = [o_ph + 92160 + i * 2048 for i in range(4)]

        cos_s = view(o_cos, 128, 1024, BF16)
        sin_s = view(o_sin, 128, 1024, BF16)
        idn_s = view(o_idn, 128, 128, BF16)
        btb_s = view(o_btb, 128, NSLOT, F32)
        kidx_s = view(o_kidx, 128, G * NSLOT, I32)
        bq_s = view(o_bq, 128, 32, F32)
        bk_s = view(o_bk, 128, 8, F32)
        bv_s = view(o_bv, 1, 1024, BF16)
        bo_s = view(o_bo, 128, 32, F32)
        bd_s = view(o_bd, 128, 32, F32)
        bu_s = view(o_bu, 128, HC, F32)
        bf_s = view(o_bf, 128, 32, F32)
        eps_s = view(o_eps, 128, 1, F32)
        ones_c = view(o_on1, 128, 1, BF16)
        ones_r = view(o_on2, 1, 128, BF16)
        onesf_r = view(o_on3, 1, 128, F32)
        lau_s = view(o_lau, 128, 32 * 16, BF16)
        lad_s = view(o_lad, 128, HC * 16, BF16)
        st_s = view(o_st, 128, 8, F32)
        wgr = [view(o, 128, 512, BF16) for o in o_wgr]
        rtmp = [view(o, 128, 512, F32) for o in o_rt]
        cev_s = view(o_cev, 128, 1024, BF16)
        fev = [view(o, 128, 512, F32) for o in o_fev]
        qbf = [view(o, 128, 1024, BF16) for o in o_qbf]
        ptil = [view(o, 128, 1024, BF16) for o in o_ptl]
        rrep_s = view(o_rrp, 128, 1024, F32)
        sums_s = view(o_sum, 1, 1024, F32)
        rinv_s = view(o_rin, 1, 1024, F32)
        tl_s = view(o_tl, 16, 512, BF16)
        t2_s = view(o_t2, 16, 512, BF16)
        lbur = [view(o, 16, 512, BF16) for o in o_lbu]
        lbdr = [view(o, 16, 512, BF16) for o in o_lbd]
        xn_a = view(o_xna, 128, 32 * 512, BF16)
        xn_b = view(o_xnb, 128, 32 * 512, BF16)

        xt_s = view(o_ph, 128, D, F32)
        xq_s = view(o_ph + 16384, 128, D, BF16)
        sq_s = view(o_ph + 24576, 128, D, F32)
        dq_off = [o_ph, o_ph + 24576 + 512]
        dqc = [view(dq_off[i], 128, 2048, I32) for i in range(2)]
        dq16 = [view(dq_off[i] + 8192, 128, 2048, F16) for i in range(2)]
        dqa = [view(dq_off[i] + 12288, 128, 2048, F16) for i in range(2)]
        dqt = [view(dq_off[i] + 16384, 128, 2048, F16) for i in range(2)]
        dqw = [view(dq_off[i] + 20480, 128, 2048, BF16) for i in range(2)]
        dqm = [view(dq_off[i] + 24576, 128, 64, F32) for i in range(2)]
        kst = [view(o_ph + i * 12288, 128, NSLOT * 128, BF16) for i in range(2)]
        vst = [view(o_ph + 24576 + i * 12288, 128, NSLOT * 128, BF16) for i in range(2)]
        msk_s = view(o_ph + 49152, 128, 4 * 512, BF16)
        ctv = [view(o_ph + 88064 + i * 1024, 128, 512, BF16) for i in range(4)]
        x2c = [view(o_ph + i * 1024, 128, 512, BF16) for i in range(HC)]
        assert o_ph + 92160 + 8192 <= HEAPB

        psA = ctx.enter_context(nc.psum_tensor("psA", [128, 1024], F32))
        psB = ctx.enter_context(nc.psum_tensor("psB", [128, 1024], F32))
        ps4 = ctx.enter_context(nc.psum_tensor("ps4", [128, 512], F32))
        ps5 = ctx.enter_context(nc.psum_tensor("ps5", [128, 512], F32))
        ps6 = ctx.enter_context(nc.psum_tensor("ps6", [128, 512], F32))
        ps7 = ctx.enter_context(nc.psum_tensor("ps7", [128, 512], F32))
        pbank = [psA.ap()[:, 0:512], psA.ap()[:, 512:1024],
                 psB.ap()[:, 0:512], psB.ap()[:, 512:1024],
                 ps4.ap()[:], ps5.ap()[:], ps6.ap()[:], ps7.ap()[:]]

        R = lambda n: Res(n)
        r_const = R("const")
        r_xt, r_xq, r_st = R("xt"), R("xq"), R("st")
        r_xna, r_xnb = R("xna"), R("xnb")
        r_wgr = [R(f"wgr{i}") for i in range(8)]
        r_rt0, r_rt1 = R("rt0"), R("rt1")
        r_cev = R("cev")
        r_fev = [R(f"fev{i}") for i in range(4)]
        r_qbf = [R(f"qbf{i}") for i in range(4)]
        r_ptl = [R(f"ptl{i}") for i in range(4)]
        r_rrp, r_sumv, r_rin = R("rrp"), R("sumv"), R("rin")
        r_tl, r_t2 = R("tl"), R("t2")
        r_lbur = [R(f"lbur{i}") for i in range(2)]
        r_lbdr = [R(f"lbdr{i}") for i in range(2)]
        r_dq = [R(f"dq{i}") for i in range(2)]
        r_dqc = [R(f"dqc{i}") for i in range(2)]
        r_dqm = [R(f"dqm{i}") for i in range(2)]
        r_kst = [R(f"kst{i}") for i in range(2)]
        r_vst = [R(f"vst{i}") for i in range(2)]
        r_msk = R("msk")
        r_ctv = [R(f"ctv{i}") for i in range(4)]
        r_x2 = [R(f"x2_{i}") for i in range(HC)]
        r_ps = [R(f"ps{i}") for i in range(8)]
        r_psA, r_psB = r_ps[0], r_ps[2]
        r_kown, r_vown, r_kg, r_vg = R("kown"), R("vown"), R("kg"), R("vg")
        r_wuo, r_wdo, r_wug, r_wdg = R("wuo"), R("wdo"), R("wug"), R("wdg")
        r_ctxd, r_atnd = R("ctxd"), R("atnd")

        def bres(i):
            if i < 2:
                return r_ps[0]
            if i < 4:
                return r_ps[2]
            return r_ps[i]

        LDs, LGs, LAs, DVs, ACs, PEs, POs, CCs, OTs = (sem_h[s] for s in ALL_SEMS)
        SH = sem_h

        def wrap(body, emit_w, sem, inc):
            def fn(e):
                for s, c in emit_w.items():
                    e.wait_ge(SH[s], c)
                body(e)
            return fn

        def dstep(eng, reads, writes, mk, n_dma=1):
            sem = ESEM[eng][1]

            def body(e, mk=mk, sem=sem):
                mk(e, lambda inst: inst.then_inc(SH[sem], 16))
            return b.step(eng, sem, 16 * n_dma, reads, writes, body)

        def cstep(eng, reads, writes, mk):
            sem = ESEM[eng][0]

            def body(e, mk=mk, sem=sem):
                last = mk(e)
                last.then_inc(SH[sem], 1)
            return b.step(eng, sem, 1, reads, writes, body)


        class _G:
            pass
        try:
            def ld1(dst, src):
                return dstep("sync", [], [r_const],
                             lambda e, inc, d=dst, s=src: inc(e.dma_start(out=d, in_=s)))
            ld1(cos_s, cosq); ld1(sin_s, sinq); ld1(idn_s, ident_i)
            ld1(btb_s, btbl_i); ld1(kidx_s, kidx_i)
            ld1(bq_s, bq_i); ld1(bk_s, bk_i); ld1(bo_s, bo_i); ld1(bd_s, bd_i)
            ld1(bu_s, bu_i)
            dstep("gpsimd", [], [r_const],
                  lambda e, inc: inc(e.dma_start(out=bv_s, in_=bv_i)))
            dstep("gpsimd", [], [r_const],
                  lambda e, inc: inc(e.dma_start(
                      out=lau_s.rearrange("p (c j) -> p c j", c=32),
                      in_=la_u.rearrange("(c p) j -> p c j", p=128))))
            dstep("gpsimd", [], [r_const],
                  lambda e, inc: inc(e.dma_start(
                      out=lad_s.rearrange("p (c j) -> p c j", c=HC),
                      in_=la_d.rearrange("(c p) j -> p c j", p=128))))
    
            def _ones(v):
                v.memset(ones_c, 1.0)
                v.memset(ones_r, 1.0)
                v.memset(onesf_r, 1.0)
                v.memset(eps_s, EPS)
                return v.tensor_add(out=bf_s, in0=bo_s, in1=bd_s)
            cstep("vector", [r_const], [r_const], _ones)
    
            if KTRUNC < 1:
                raise _Done
            for i in range(8):
                xnr = r_xna if i < 4 else r_xnb
                xnv = xn_a if i < 4 else xn_b
                toff = (i % 4) * 128
                dstep("sync", [], [r_xt],
                      lambda e, inc, i=i: inc(e.dma_start(out=xt_s, in_=x_own[i * 128:(i + 1) * 128, :])))
    
                cstep("vector", [r_xt], [r_st],
                      lambda v: v.reduce_sum(out=st_s[:, 0:1], in_=xt_s, axis=AX.X))
                cstep("gpsimd", [r_st], [r_st],
                      lambda g: g.tensor_scalar(out=st_s[:, 1:2], in0=st_s[:, 0:1],
                                                scalar1=1.0 / D, scalar2=None, op0=OP.mult))
    
                def _sq(v):
                    v.tensor_tensor(out=sq_s, in0=xt_s, in1=xt_s, op=OP.mult)
                    return v.reduce_sum(out=st_s[:, 2:3], in_=sq_s, axis=AX.X)
                cstep("vector", [r_xt, r_st], [r_st], _sq)
    
                cstep("gpsimd", [r_st], [r_st],
                      lambda g: g.tensor_scalar(out=st_s[:, 3:4], in0=st_s[:, 2:3],
                                                scalar1=1.0 / D, scalar2=None, op0=OP.mult))
                cstep("vector", [r_st], [r_st],
                      lambda v: v.scalar_tensor_tensor(out=st_s[:, 4:5], in0=st_s[:, 1:2],
                                                       scalar=st_s[:, 1:2], in1=st_s[:, 3:4],
                                                       op0=OP.mult, op1=OP.subtract))
    
                cstep("scalar", [r_st], [r_st],
                      lambda sc: sc.activation(out=st_s[:, 5:6], in_=st_s[:, 4:5],
                                               func=AF.Ln, bias=eps_s, scale=-1.0))
                cstep("vector", [r_st], [r_st],
                      lambda v: v.tensor_copy(out=st_s[:, 7:8], in_=st_s[:, 5:6]))
                cstep("scalar", [r_st], [r_st],
                      lambda sc: sc.activation(out=st_s[:, 6:7], in_=st_s[:, 7:8],
                                               func=AF.Exp, scale=-0.5))
    
                def _norm(v):
                    v.tensor_scalar(out=sq_s, in0=xt_s, scalar1=st_s[:, 1:2],
                                    scalar2=None, op0=OP.subtract)
                    return v.tensor_scalar(out=xq_s, in0=sq_s, scalar1=st_s[:, 6:7],
                                           scalar2=None, op0=OP.mult)
                cstep("vector", [r_xt, r_st], [r_xq], _norm)
    
                for j in range(32):
                    bank = r_ps[6] if j % 2 == 0 else r_ps[7]
                    pv = (ps6 if j % 2 == 0 else ps7).ap().bitcast(BF16)[:, 0:128]
                    cstep("tensor", [r_xq, r_const], [bank],
                          lambda t, j=j, pv=pv: t.transpose(pv, xq_s[:, j * 128:(j + 1) * 128], idn_s))
                    cstep("vector", [bank], [xnr],
                          lambda v, j=j, pv=pv, xnv=xnv, toff=toff:
                          v.tensor_copy(out=xnv[:, j * 512 + toff: j * 512 + toff + 128], in_=pv))
    
            if KTRUNC < 2:
                raise _Done
            ring = [0]
    
            def next_ring():
                i = ring[0] % 8
                ring[0] += 1
                return i
    
            def xslice(d, th):
                return (xn_a if th == 0 else xn_b)[:, d * 512:(d + 1) * 512]
    
            for gq in range(2):
                grans = []
                for d in range(32):
                    i = next_ring()
                    dstep("gpsimd", [], [r_wgr[i]],
                          lambda e, inc, d=d, gq=gq, i=i: inc(e.dma_start(
                              out=wgr[i], in_=Wk[d * 128:(d + 1) * 128, gq * 512:(gq + 1) * 512])))
                    def _mmk(t, i=i, d=d):
                        last = None
                        for sub in range(4):
                            for th in range(2):
                                last = t.matmul(pbank[sub * 2 + th], wgr[i][:, sub * 128:(sub + 1) * 128],
                                                xslice(d, th), start=(d == 0), stop=(d == 31))
                        return last
                    cstep("tensor", [r_wgr[i], r_xna, r_xnb],
                          [bres(k) for k in range(8)], _mmk)
                if KDBG and gq == 1:
                    cstep("vector", [r_ps[0]], [r_fev[0]],
                          lambda v: v.tensor_copy(out=fev[0], in_=pbank[0]))
                    dstep("sync", [r_fev[0]], [],
                          lambda e, inc: inc(e.dma_start(out=dbg_kps, in_=fev[0])))
                for sub in range(4):
                    g = gq * 4 + sub
                    for th in range(2):
                        bi = sub * 2 + th
                        cs = cos_s[:, th * 512:(th + 1) * 512]
                        ss = sin_s[:, th * 512:(th + 1) * 512]
    
                        cstep("vector", [bres(bi), r_const], [r_rt0],
                              lambda v, bi=bi, g=g: v.tensor_scalar(
                                  out=rtmp[0], in0=pbank[bi], scalar1=bk_s[:, g:g + 1],
                                  scalar2=None, op0=OP.add))
    
                        if "w" not in KSW:
                            def _swp(e, inc):
                                inc(e.dma_start(out=rtmp[1][0:64, :], in_=rtmp[0][64:128, :]))
                                inc(e.dma_start(out=rtmp[1][64:128, :], in_=rtmp[0][0:64, :]))
                            dstep("sync", [r_rt0], [r_rt1], _swp, n_dma=2)

                        if "r" not in KSW:
                            def _ropek(v, cs=cs, ss=ss):
                                v.tensor_tensor(out=rtmp[0], in0=rtmp[0], in1=cs, op=OP.mult)
                                v.tensor_tensor(out=rtmp[1], in0=rtmp[1], in1=ss, op=OP.mult)
                                return v.tensor_add(out=cev_s[:, 0:512], in0=rtmp[0], in1=rtmp[1])
                            cstep("vector", [r_rt0, r_rt1, r_const], [r_rt0, r_rt1, r_cev], _ropek)
                        else:
                            cstep("vector", [r_rt0], [r_cev],
                                  lambda v: v.tensor_copy(out=cev_s[:, 0:512], in_=rtmp[0]))
                        dst = k_own.ap().rearrange("(g bl kd) u -> g kd bl u", g=G, bl=8)[g, :, th * 4:th * 4 + 4, :]
                        src = cev_s[:, 0:512].rearrange("kd (bl u) -> kd bl u", bl=4)
                        if "s" not in KSW:
                            dstep("sync", [r_cev], [r_kown],
                                  lambda e, inc, dst=dst, src=src: inc(
                                      e.dma_start(out=dst, in_=src)))
    
            if KT2 < 1:
                raise _Done
            for vh in range(2):
                grans = []
                for d in range(32):
                    i = next_ring()
                    dstep("gpsimd", [], [r_wgr[i]],
                          lambda e, inc, d=d, vh=vh, i=i: inc(e.dma_start(
                              out=wgr[i], in_=Wv[d * 128:(d + 1) * 128, vh * 512:(vh + 1) * 512])))
                    def _mmv(t, i=i, d=d):
                        last = None
                        for tc in range(8):
                            xnh = xn_a if tc < 4 else xn_b
                            toff = (tc % 4) * 128
                            last = t.matmul(pbank[tc], xnh[:, d * 512 + toff: d * 512 + toff + 128],
                                            wgr[i], start=(d == 0), stop=False)
                        return last
                    cstep("tensor", [r_wgr[i], r_xna, r_xnb],
                          [bres(k) for k in range(8)], _mmv)
    
                def _mvb(t, vh=vh):
                    last = None
                    for tc in range(8):
                        last = t.matmul(pbank[tc], ones_r, bv_s[:, vh * 512:(vh + 1) * 512],
                                        start=False, stop=True)
                    return last
                cstep("tensor", [r_const], [bres(k) for k in range(8)], _mvb)
                for tc in range(8):
                    cstep("vector", [bres(tc)], [r_cev],
                          lambda v, tc=tc: v.tensor_copy(out=cev_s[:, 0:512], in_=pbank[tc]))
                    dst = v_own.ap().rearrange("(g bl u) k -> bl u g k", g=G, bl=8)[tc, :, vh * 4:vh * 4 + 4, :]
                    src = cev_s[:, 0:512].rearrange("u (g k) -> u g k", g=4)
                    dstep("sync", [r_cev], [r_vown],
                          lambda e, inc, dst=dst, src=src: inc(e.dma_start(out=dst, in_=src)))
    
            if KT2 < 2:
                raise _Done
            def _cck(e):
                e.collective_compute("AllGather", OP.bypass,
                                     replica_groups=[[0, 1, 2, 3], [4, 5, 6, 7]],
                                     ins=[k_own.ap().opt()], outs=[k_gth.ap().opt()]).then_inc(SH["CC"], 1)
            b.step("gpsimd", "CC", 1, [r_kown], [r_kg], _cck)
    
            def _ccv(e):
                e.collective_compute("AllGather", OP.bypass,
                                     replica_groups=[[0, 1, 2, 3], [4, 5, 6, 7]],
                                     ins=[v_own.ap().opt()], outs=[v_gth.ap().opt()]).then_inc(SH["CC"], 1)
            b.step("gpsimd", "CC", 1, [r_vown], [r_vg], _ccv)
    
            if KTRUNC < 3:
                raise _Done
            gate_p3 = b.snapshot()
            for rr in (r_dq + r_dqc + r_dqm):
                rr.init = dict(gate_p3)
            units = []
            for hcl in range(UP_PAD):
                for sub2 in range(2):
                    units.append(("u", hcl, sub2))
            for dr in range(4):
                for sub2 in range(6):
                    units.append(("d", dr, sub2))
            for ui, (kind, a, sub2) in enumerate(units):
                en = ui % 2
                if kind == "u":
                    w = 2048
                    csrc = cod_u[a * 128:(a + 1) * 128, sub2 * 2048:(sub2 + 1) * 2048]
                    msrc = amx_u[a * 128:(a + 1) * 128, sub2 * 32:(sub2 + 1) * 32]
                    nblk = 32
                    wdst = wu_own.ap()[a * 128:(a + 1) * 128, sub2 * 2048:(sub2 + 1) * 2048]
                    wres = r_wuo
                else:
                    w = 2048 if sub2 < 5 else 768
                    csrc = cod_d[a * 128:(a + 1) * 128, sub2 * 2048: sub2 * 2048 + w]
                    msrc = amx_d[a * 128:(a + 1) * 128, sub2 * 32: sub2 * 32 + w // 64]
                    nblk = w // 64
                    wdst = wd_own.ap()[a * 128:(a + 1) * 128, sub2 * 2048: sub2 * 2048 + w]
                    wres = r_wdo
                eng = "vector" if w == 2048 else "gpsimd"
                dstep("sync", [], [r_dqc[en]],
                      lambda e, inc, en=en, w=w, csrc=csrc: inc(
                          e.dma_start(out=dqc[en][:, 0:w], in_=csrc)))
                dstep("sync", [], [r_dqm[en]],
                      lambda e, inc, en=en, nblk=nblk, msrc=msrc: inc(
                          e.dma_start(out=dqm[en][:, 0:nblk], in_=msrc)))
    
                def _dq(e, en=en, w=w, nblk=nblk):
                    e.tensor_copy(out=dq16[en][:, 0:w], in_=dqc[en][:, 0:w])
                    e.memset(dqa[en][:, 0:w], 0.0)
                    for kk in range(16):
                        e.tensor_scalar(out=dqt[en][:, 0:w], in0=dq16[en][:, 0:w],
                                        scalar1=float(kk), scalar2=float(NF4[kk]),
                                        op0=OP.is_equal, op1=OP.mult)
                        e.tensor_tensor(out=dqa[en][:, 0:w], in0=dqa[en][:, 0:w],
                                        in1=dqt[en][:, 0:w], op=OP.add)
                    return e.tensor_tensor(
                        out=dqw[en][:, 0:w].rearrange("p (a c) -> p a c", c=64),
                        in0=dqa[en][:, 0:w].rearrange("p (a c) -> p a c", c=64),
                        in1=dqm[en][:, 0:nblk].unsqueeze(2).to_broadcast([128, nblk, 64]),
                        op=OP.mult)
                cstep(eng, [r_dqc[en], r_dqm[en]], [r_dq[en]], _dq)
                dstep("scalar", [r_dq[en]], [wres],
                      lambda e, inc, en=en, w=w, wdst=wdst: inc(
                          e.dma_start(out=wdst, in_=dqw[en][:, 0:w])))
    
            def _ccwu(e):
                e.collective_compute("AllGather", OP.bypass,
                                     replica_groups=[list(range(8))],
                                     ins=[wu_own.ap().opt()], outs=[wu_gth.ap().opt()]).then_inc(SH["CC"], 1)
            b.step("gpsimd", "CC", 1, [r_wuo], [r_wug], _ccwu)
    
            def _ccwd(e):
                e.collective_compute("AllGather", OP.bypass,
                                     replica_groups=[list(range(8))],
                                     ins=[wd_own.ap().opt()], outs=[wd_gth.ap().opt()]).then_inc(SH["CC"], 1)
            b.step("gpsimd", "CC", 1, [r_wdo], [r_wdg], _ccwd)
    
            if KTRUNC < 4:
                raise _Done
            gate_p4 = b.snapshot()
            for rr in (r_kst + r_vst + [r_msk] + r_qbf + r_ptl + [r_rrp, r_sumv, r_rin]):
                rr.init = dict(gate_p4)
            dstep("sync", [], [r_msk],
                  lambda e, inc: inc(e.dma_start(out=msk_s, in_=masks_i)))
    
            for g in range(G):
                sb = g % 2
                for s in range(NSLOT):
                    idx = kidx_s[:, g * NSLOT + s: g * NSLOT + s + 1]
    
                    def _stg(e, inc, s=s, sb=sb, idx=idx):
                        inc(e.indirect_dma_start(kst[sb][:, s * 128:(s + 1) * 128], None,
                                                 k_gth.ap(), IndirectOffsetOnAxis(idx, 0)))
                        inc(e.indirect_dma_start(vst[sb][:, s * 128:(s + 1) * 128], None,
                                                 v_gth.ap(), IndirectOffsetOnAxis(idx, 0)))
                    dstep("gpsimd", [r_kg, r_vg, r_const], [r_kst[sb], r_vst[sb]], _stg, n_dma=2)
    
                grans = []
                for d in range(32):
                    i = next_ring()
                    dstep("gpsimd", [], [r_wgr[i]],
                          lambda e, inc, d=d, g=g, i=i: inc(e.dma_start(
                              out=wgr[i], in_=Wq[d * 128:(d + 1) * 128, g * 512:(g + 1) * 512])))
                    def _mmq(t, i=i, d=d):
                        last = None
                        for rh in range(4):
                            for th in range(2):
                                last = t.matmul(pbank[rh * 2 + th], wgr[i][:, rh * 128:(rh + 1) * 128],
                                                xslice(d, th), start=(d == 0), stop=(d == 31))
                        return last
                    cstep("tensor", [r_wgr[i], r_xna, r_xnb],
                          [bres(k) for k in range(8)], _mmq)
                for rh in range(4):
                    h = g * 4 + rh
                    for th in range(2):
                        bi = rh * 2 + th
                        cs = cos_s[:, th * 512:(th + 1) * 512]
                        ss = sin_s[:, th * 512:(th + 1) * 512]
    
                        cstep("vector", [bres(bi), r_const], [r_rt0],
                              lambda v, bi=bi, h=h: v.tensor_scalar(
                                  out=rtmp[0], in0=pbank[bi], scalar1=bq_s[:, h:h + 1],
                                  scalar2=None, op0=OP.add))
    
                        def _swpq(e, inc):
                            inc(e.dma_start(out=rtmp[1][0:64, :], in_=rtmp[0][64:128, :]))
                            inc(e.dma_start(out=rtmp[1][64:128, :], in_=rtmp[0][0:64, :]))
                        dstep("sync", [r_rt0], [r_rt1], _swpq, n_dma=2)
    
                        def _ropeq(v, cs=cs, ss=ss, rh=rh, th=th):
                            v.tensor_tensor(out=rtmp[0], in0=rtmp[0], in1=cs, op=OP.mult)
                            v.tensor_tensor(out=rtmp[1], in0=rtmp[1], in1=ss, op=OP.mult)
                            return v.tensor_add(out=qbf[rh][:, th * 512:(th + 1) * 512],
                                                in0=rtmp[0], in1=rtmp[1])
                        cstep("vector", [r_rt0, r_rt1, r_const], [r_rt0, r_rt1, r_qbf[rh]], _ropeq)
    
                for rh in range(4):
                    h = g * 4 + rh
                    for ch in range(2):
                        base = 0 if ch == 0 else NSL_LO
                        npair = (NSL_LO if ch == 0 else NSL_HI) // 2
                        ctx_r = r_ps[4] if ch == 0 else r_ps[5]
                        ctx_b = ps4 if ch == 0 else ps5
                        for pr in range(npair):
                            s0 = base + 2 * pr
                            sc_r = r_psA if pr % 2 == 0 else r_psB
                            sc_b = psA if pr % 2 == 0 else psB
    
                            def _mm1(t, s0=s0, sc_b=sc_b, rh=rh, ch=ch, sb=sb):
                                last = None
                                for k in range(2):
                                    last = t.matmul(sc_b.ap()[:, k * 512:(k + 1) * 512],
                                                    kst[sb][:, (s0 + k) * 128:(s0 + k + 1) * 128],
                                                    qbf[rh][:, ch * 512:(ch + 1) * 512],
                                                    start=True, stop=True)
                                return last
                            cstep("tensor", [r_kst[sb], r_qbf[rh]], [sc_r], _mm1)
    
                            pi = (ch * 16 + pr) % 4
    
                            def _exp(sc, sc_b=sc_b, pi=pi, s0=s0):
                                return sc.activation(out=ptil[pi], in_=sc_b.ap(),
                                                     func=AF.Exp, bias=btb_s[:, s0:s0 + 1],
                                                     scale=SCALE)
                            cstep("scalar", [sc_r, r_const], [r_ptl[pi]], _exp)
    
                            if pr < 2:
                                def _msk2(v, pi=pi, pr=pr):
                                    v.tensor_tensor(out=ptil[pi][:, 0:512],
                                                    in0=ptil[pi][:, 0:512],
                                                    in1=msk_s[:, 2 * pr * 512:(2 * pr + 1) * 512],
                                                    op=OP.mult)
                                    return v.tensor_tensor(out=ptil[pi][:, 512:1024],
                                                           in0=ptil[pi][:, 512:1024],
                                                           in1=msk_s[:, (2 * pr + 1) * 512:(2 * pr + 2) * 512],
                                                           op=OP.mult)
                                cstep("vector", [r_msk], [r_ptl[pi]], _msk2)
    
                            def _mm2(t, pi=pi, s0=s0, pr=pr, npair=npair, ctx_b=ctx_b, sb=sb):
                                for k in range(2):
                                    t.matmul(ps6.ap()[0:1, 0:512], ones_c,
                                             ptil[pi][:, k * 512:(k + 1) * 512],
                                             start=(pr == 0 and k == 0),
                                             stop=(pr == npair - 1 and k == 1))
                                last = None
                                for k in range(2):
                                    last = t.matmul(ctx_b.ap(),
                                                    vst[sb][:, (s0 + k) * 128:(s0 + k + 1) * 128],
                                                    ptil[pi][:, k * 512:(k + 1) * 512],
                                                    start=(pr == 0 and k == 0),
                                                    stop=(pr == npair - 1 and k == 1))
                                return last
                            cstep("tensor", [r_ptl[pi], r_vst[sb], r_const],
                                  [r_ps[6], ctx_r], _mm2)
                        cstep("scalar", [r_ps[6]], [r_sumv],
                              lambda sc, ch=ch: sc.copy(out=sums_s[:, ch * 512:(ch + 1) * 512],
                                                        in_=ps6.ap()[0:1, 0:512]))
                        cstep("vector", [r_sumv], [r_rin],
                              lambda v, ch=ch: v.reciprocal(out=rinv_s[:, ch * 512:(ch + 1) * 512],
                                                            in_=sums_s[:, ch * 512:(ch + 1) * 512]))
                    for ch in range(2):
                        cstep("tensor", [r_rin, r_const], [r_ps[7]],
                              lambda t, ch=ch: t.matmul(ps7.ap(), onesf_r,
                                                        rinv_s[:, ch * 512:(ch + 1) * 512],
                                                        start=True, stop=True))
                        cstep("vector", [r_ps[7]], [r_rrp],
                              lambda v, ch=ch: v.tensor_copy(out=rrep_s[:, ch * 512:(ch + 1) * 512],
                                                             in_=ps7.ap()))
    
                    def _cev2(v):
                        v.tensor_tensor(out=cev_s[:, 0:512], in0=ps4.ap(),
                                        in1=rrep_s[:, 0:512], op=OP.mult)
                        return v.tensor_tensor(out=cev_s[:, 512:1024], in0=ps5.ap(),
                                               in1=rrep_s[:, 512:1024], op=OP.mult)
                    cstep("vector", [r_ps[4], r_ps[5], r_rrp], [r_cev], _cev2)
                    dstep("sync", [r_cev], [r_ctxd],
                          lambda e, inc, h=h: inc(e.dma_start(
                              out=ctx_dr.ap()[h * 128:(h + 1) * 128, :], in_=cev_s)))
    
            if KTRUNC < 5:
                raise _Done
            gate_p5 = b.snapshot()
            for rr in r_ctv:
                rr.init = dict(gate_p5)
            for dq_ in range(8):
                grans = []
                for cd in range(32):
                    i = next_ring()
                    dstep("gpsimd", [], [r_wgr[i]],
                          lambda e, inc, cd=cd, dq_=dq_, i=i: inc(e.dma_start(
                              out=wgr[i], in_=Wo[cd * 128:(cd + 1) * 128, dq_ * 512:(dq_ + 1) * 512])))
                    ci = cd % 2
    
                    def _ctl(e, inc, cd=cd, ci=ci):
                        inc(e.dma_start(out=ctv[2 * ci], in_=ctx_dr.ap()[cd * 128:(cd + 1) * 128, 0:512]))
                        inc(e.dma_start(out=ctv[2 * ci + 1], in_=ctx_dr.ap()[cd * 128:(cd + 1) * 128, 512:1024]))
                    dstep("sync", [r_ctxd], [r_ctv[2 * ci], r_ctv[2 * ci + 1]], _ctl, n_dma=2)

                    def _mmo(t, i=i, cd=cd, ci=ci):
                        last = None
                        for sub in range(4):
                            for th in range(2):
                                last = t.matmul(pbank[sub * 2 + th], wgr[i][:, sub * 128:(sub + 1) * 128],
                                                ctv[2 * ci + th], start=(cd == 0), stop=(cd == 31))
                        return last
                    cstep("tensor", [r_wgr[i], r_ctv[2 * ci], r_ctv[2 * ci + 1]],
                          [bres(k) for k in range(8)], _mmo)
                for sub in range(4):
                    dout = dq_ * 4 + sub
                    for th in range(2):
                        fi = 2 * (sub % 2) + th
                        cstep("vector", [bres(sub * 2 + th)], [r_fev[fi]],
                              lambda v, sub=sub, th=th, fi=fi: v.tensor_copy(
                                  out=fev[fi], in_=pbank[sub * 2 + th]))
                        dstep("sync", [r_fev[fi]], [r_atnd],
                              lambda e, inc, dout=dout, th=th, fi=fi: inc(e.dma_start(
                                  out=atn_dr.ap()[dout * 128:(dout + 1) * 128,
                                                  th * 512:(th + 1) * 512], in_=fev[fi])))
    
            if KTRUNC < 6:
                raise _Done
            gate_p6 = b.snapshot()
            for rr in r_x2:
                rr.init = dict(gate_p6)
            out_sem_total = [0]
            for th in range(2):
                def _tlmm(t, th=th):
                    last = None
                    for d in range(32):
                        last = t.matmul(ps6.ap()[0:16, 0:512], lau_s[:, d * 16:(d + 1) * 16],
                                        xslice(d, th), start=(d == 0), stop=(d == 31))
                    return last
                cstep("tensor", [r_const, r_xna, r_xnb], [r_ps[6]], _tlmm)
                cstep("vector", [r_ps[6]], [r_tl],
                      lambda v: v.tensor_copy(out=tl_s, in_=ps6.ap()[0:16, 0:512]))
    
                if KT6 < 1:
                    raise _Done
                hquads = []
                for r in range(8):
                    qsz = [4, 4, UP_N[r] - 8]
                    base = 0
                    for qs in qsz:
                        hquads.append((r, base, qs))
                        base += qs
                brot = [0]
                for (r, hbase, qs) in hquads:
                    li = (hbase // 4 + r) % 2
                    hg0 = up_glob(r, hbase)
                    dstep("gpsimd", [], [r_lbur[li]],
                          lambda e, inc, hg0=hg0, qs=qs, li=li: inc(e.dma_start(
                              out=lbur[li][:, 0:qs * 128],
                              in_=lb_u[:, hg0 * 128:(hg0 + qs) * 128])))
                    banks = [(brot[0] + j) % 8 for j in range(qs)]
                    for d in range(32):
                        i = next_ring()
                        src = wu_gth.ap()[r * UP_PAD * 128 + hbase * 128:
                                          r * UP_PAD * 128 + (hbase + qs) * 128,
                                          d * 128:(d + 1) * 128]
                        dstep("sync", [r_wug], [r_wgr[i]],
                              lambda e, inc, src=src, i=i, qs=qs: inc(
                                  e.dma_start_transpose(wgr[i][:, 0:qs * 128], src)))

                        def _mmu(t, i=i, d=d, qs=qs, banks=banks, th=th):
                            last = None
                            for j in range(qs):
                                last = t.matmul(pbank[banks[j]], wgr[i][:, j * 128:(j + 1) * 128],
                                                xslice(d, th), start=(d == 0), stop=False)
                            return last
                        cstep("tensor", [r_wgr[i], r_xna, r_xnb],
                              list({bres(x) for x in banks}), _mmu)
    
                    def _mlu(t, qs=qs, banks=banks, li=li):
                        last = None
                        for j in range(qs):
                            last = t.matmul(pbank[banks[j]], lbur[li][:, j * 128:(j + 1) * 128],
                                            tl_s, start=False, stop=True)
                        return last
                    cstep("tensor", [r_tl, r_lbur[li]], list({bres(x) for x in banks}), _mlu)
                    for j in range(qs):
                        hg = up_glob(r, hbase + j)
                        cstep("scalar", [bres(banks[j]), r_const], [r_x2[hg]],
                              lambda sc, hg=hg, bj=banks[j]: sc.activation(
                                  out=x2c[hg], in_=pbank[bj], func=AF.Relu,
                                  bias=bu_s[:, hg:hg + 1]))
                    brot[0] = (brot[0] + qs) % 8
    
                if KT6 < 2:
                    raise _Done
                def _t2mm(t):
                    last = None
                    for hc_ in range(HC):
                        last = t.matmul(ps6.ap()[0:16, 0:512], lad_s[:, hc_ * 16:(hc_ + 1) * 16],
                                        x2c[hc_], start=(hc_ == 0), stop=(hc_ == HC - 1))
                    return last
                cstep("tensor", [r_const] + r_x2, [r_ps[6]], _t2mm)
                cstep("vector", [r_ps[6]], [r_t2],
                      lambda v: v.tensor_copy(out=t2_s, in_=ps6.ap()[0:16, 0:512]))
    
                if KT6 < 3:
                    raise _Done
                for r in range(8):
                    banks = [(4 * (r % 2)) + j for j in range(4)]
                    li = r % 2
                    dstep("gpsimd", [], [r_lbdr[li]],
                          lambda e, inc, r=r, li=li: inc(e.dma_start(
                              out=lbdr[li][:, 0:512], in_=lb_d[:, r * 512:(r + 1) * 512])))
                    for hc_ in range(HC):
                        i = next_ring()
                        src = wd_gth.ap()[r * 512:(r + 1) * 512, hc_ * 128:(hc_ + 1) * 128]
                        dstep("scalar", [r_wdg], [r_wgr[i]],
                              lambda e, inc, src=src, i=i: inc(e.dma_start_transpose(wgr[i], src)))

                        def _mmd(t, i=i, hc_=hc_, banks=banks):
                            last = None
                            for j in range(4):
                                last = t.matmul(pbank[banks[j]], wgr[i][:, j * 128:(j + 1) * 128],
                                                x2c[hc_], start=(hc_ == 0), stop=False)
                            return last
                        cstep("tensor", [r_wgr[i], r_x2[hc_]],
                              list({bres(x) for x in banks}), _mmd)
    
                    def _mld(t, r=r, banks=banks, li=li):
                        last = None
                        for j in range(4):
                            last = t.matmul(pbank[banks[j]], lbdr[li][:, j * 128:(j + 1) * 128],
                                            t2_s, start=False, stop=True)
                        return last
                    cstep("tensor", [r_t2, r_lbdr[li]], list({bres(x) for x in banks}), _mld)
                    for j in range(4):
                        if "f" in KD:
                            continue
                        dout = r * 4 + j
                        fa = 2 * (j % 2)
    
                        def _fl(e, inc, dout=dout, th=th, fa=fa):
                            inc(e.dma_start(out=fev[fa],
                                            in_=atn_dr.ap()[dout * 128:(dout + 1) * 128,
                                                            th * 512:(th + 1) * 512]))
                            inc(e.dma_start(out=fev[fa + 1],
                                            in_=x_ownT[dout * 128:(dout + 1) * 128,
                                                       th * 512:(th + 1) * 512]))
                        dstep("sync", [r_atnd], [r_fev[fa], r_fev[fa + 1]], _fl, n_dma=2)
    
                        if "v" not in KD:
                            def _fin(v, dout=dout, fa=fa, bj=banks[j]):
                                v.scalar_tensor_tensor(out=fev[fa], in0=pbank[bj],
                                                       scalar=bf_s[:, dout:dout + 1],
                                                       in1=fev[fa], op0=OP.add, op1=OP.add)
                                return v.tensor_add(out=fev[fa], in0=fev[fa], in1=fev[fa + 1])
                            cstep("vector", [bres(banks[j]), r_fev[fa + 1], r_const],
                                  [r_fev[fa]], _fin)
                        else:
                            cstep("vector", [bres(banks[j]), r_fev[fa + 1], r_const],
                                  [r_fev[fa]],
                                  lambda v, fa=fa: v.tensor_copy(out=fev[fa], in_=fev[fa + 1]))
    
                        def _fst(e, inc, dout=dout, th=th, fa=fa):
                            inc(e.dma_start(out=out_fm[dout * 128:(dout + 1) * 128,
                                                       th * 512:(th + 1) * 512],
                                            in_=fev[fa]))
                        b.step("sync", "OT", 16, [r_fev[fa]], [],
                               lambda e, fa=fa, dout=dout, th=th: e.dma_start(
                                   out=out_fm[th * D + dout * 128: th * D + (dout + 1) * 128, :],
                                   in_=fev[fa]).then_inc(SH["OT"], 16))
                        out_sem_total[0] += 16
    
            if KDBG:
                def _dbg(e, inc):
                    inc(e.dma_start(out=dbg_kg, in_=k_gth.ap()))
                    inc(e.dma_start(out=dbg_vg, in_=v_gth.ap()))
                    inc(e.dma_start(out=dbg_ctx, in_=ctx_dr.ap()))
                    inc(e.dma_start(out=dbg_atn, in_=atn_dr.ap()))
                    inc(e.dma_start(out=dbg_wu, in_=wu_own.ap()))
                dstep("sync", [r_kg, r_vg, r_ctxd, r_atnd, r_wuo], [], _dbg, n_dma=5)
                dstep("sync", [r_xna], [],
                      lambda e, inc: inc(e.dma_start(out=dbg_xn, in_=xn_a)))
                dstep("sync", [r_st], [],
                      lambda e, inc: inc(e.dma_start(out=dbg_st, in_=st_s)))
                def _dbg2(e, inc):
                    for hc_ in range(HC):
                        inc(e.dma_start(out=dbg_x2[:, hc_ * 512:(hc_ + 1) * 512], in_=x2c[hc_]))
                    inc(e.dma_start(out=dbg_tl[:, 0:512], in_=tl_s))
                    inc(e.dma_start(out=dbg_tl[:, 512:1024], in_=t2_s))
                dstep("sync", r_x2 + [r_tl, r_t2], [], _dbg2, n_dma=HC + 2)
            v_out = out_sem_total[0]
            b.step("sync", "LD", 0, [], [],
                   lambda e: e.wait_ge(SH["OT"], v_out))
        except _Done:
            pass

        with nc.Block() as block:
            def replay(e, eng):
                for (emit_w, body, semn, inc) in b.q[eng]:
                    for s, c in emit_w.items():
                        e.wait_ge(SH[s], c)
                    body(e)

            @block.sync
            def _(e):
                replay(e, "sync")

            @block.gpsimd
            def _(e):
                replay(e, "gpsimd")

            @block.vector
            def _(e):
                replay(e, "vector")

            @block.scalar
            def _(e):
                replay(e, "scalar")

            @block.tensor
            def _(e):
                replay(e, "tensor")

    return nc


def _core_tables(p):
    A = list(range(4 * p, 4 * p + 4))
    Bb = list(range(28 - 4 * p, 32 - 4 * p))
    own = A + Bb

    def owner_bl(ub):
        if ub < 16:
            r = ub // 4
            bl = ub - 4 * r
        else:
            r = 7 - ub // 4
            bl = ub - (28 - 4 * r) + 4
        return r, bl

    ab_lo = 4 * p
    lo = [ab_lo + k for k in range(4)] + list(range(0, ab_lo)) + [None] * (12 - ab_lo)
    ab_hi = 28 - 4 * p
    hi = [ab_hi + k for k in range(4)] + list(range(0, ab_hi)) + [None] * (4 * p)
    slots = lo + hi
    assert len(lo) == NSL_LO and len(hi) == NSL_HI

    kidx = np.zeros((128, G * NSLOT), np.int32)
    bias = np.zeros((128, NSLOT), np.float32)
    u = np.arange(128)
    for s, ub in enumerate(slots):
        if ub is None:
            row = 0
            bias[:, s] = NEGB
        else:
            r, bl = owner_bl(ub)
            row = (r * G * 8 + bl) * 128
            bias[:, s] = -SHIFT
        for g in range(G):
            base = row + (0 if ub is None else g * 8 * 128)
            kidx[:, g * NSLOT + s] = base + u
    return own, kidx, bias


def kernel(L0_attn_hidden_states, L0_attn_Wq, L0_attn_bq, L0_attn_Wk, L0_attn_bk,
           L0_attn_Wv, L0_attn_bv, L0_attn_Wo, L0_attn_bo, L0_ffn_w_up_codes,
           L0_ffn_w_up_absmax, L0_ffn_b_up, L0_ffn_w_up_lora_a, L0_ffn_w_up_lora_b,
           L0_ffn_w_down_codes, L0_ffn_w_down_absmax, L0_ffn_b_down,
           L0_ffn_w_down_lora_a, L0_ffn_w_down_lora_b):
    x = np.ascontiguousarray(L0_attn_hidden_states)
    half = 64
    inv = 1.0 / 10000.0 ** (np.arange(half) / half)
    masks = np.zeros((128, 4, 512), np.float32)
    uu = np.arange(128)[:, None]
    jj = np.arange(512)[None, :]
    for k in range(4):
        masks[:, k, :] = (jj >= 128 * k + uu).astype(np.float32)
    masks = masks.reshape(128, 4 * 512).astype(NPBF)
    ident = np.eye(128, dtype=NPBF)

    cod_u_full = L0_ffn_w_up_codes.reshape(HID, D)
    amx_u_full = L0_ffn_w_up_absmax.reshape(HID, 64)
    cod_d_full = L0_ffn_w_down_codes.reshape(D, HID)
    amx_d_full = L0_ffn_w_down_absmax.reshape(D, 172)

    in_maps = []
    meta = []
    for c in range(8):
        bb, p = c // 4, c % 4
        own, kidx, bias = _core_tables(p)
        rows = []
        for ub in own:
            rows.extend(range(ub * 128, ub * 128 + 128))
        rows = np.array(rows)
        x_own = np.ascontiguousarray(x[bb, rows, :]).astype(np.float32)
        x_ownT = np.ascontiguousarray(x_own.T)
        ang = rows.astype(np.float64)[None, :] * inv[:, None]
        cos1 = np.cos(ang)
        sin1 = np.sin(ang)
        cosq = np.concatenate([cos1, cos1], 0).astype(NPBF)
        sinq = np.concatenate([-sin1, sin1], 0).astype(NPBF)

        g0 = sum(UP_N[:c])
        n = UP_N[c]
        cu = np.zeros((UP_PAD * 128, D), np.int32)
        au = np.zeros((UP_PAD * 128, 64), np.float32)
        cu[:n * 128] = cod_u_full[g0 * 128:(g0 + n) * 128]
        au[:n * 128] = amx_u_full[g0 * 128:(g0 + n) * 128]

        im = {
            "x_own": x_own, "x_ownT": x_ownT,
            "Wq": L0_attn_Wq, "Wk": L0_attn_Wk, "Wv": L0_attn_Wv, "Wo": L0_attn_Wo,
            "bq_i": np.ascontiguousarray(L0_attn_bq.reshape(32, 128).T),
            "bk_i": np.ascontiguousarray(L0_attn_bk.reshape(8, 128).T),
            "bv_i": L0_attn_bv.reshape(1, 1024),
            "bo_i": np.ascontiguousarray(L0_attn_bo.reshape(32, 128).T),
            "bd_i": np.ascontiguousarray(L0_ffn_b_down.reshape(32, 128).T),
            "bu_i": np.ascontiguousarray(L0_ffn_b_up.reshape(HC, 128).T),
            "cosq": cosq, "sinq": sinq, "masks_i": masks, "ident_i": ident,
            "btbl_i": bias, "kidx_i": kidx,
            "cod_u": cu, "amx_u": au,
            "cod_d": np.ascontiguousarray(cod_d_full[c * 512:(c + 1) * 512]),
            "amx_d": np.ascontiguousarray(amx_d_full[c * 512:(c + 1) * 512]),
            "la_u": L0_ffn_w_up_lora_a, "lb_u": L0_ffn_w_up_lora_b,
            "la_d": L0_ffn_w_down_lora_a, "lb_d": L0_ffn_w_down_lora_b,
        }
        in_maps.append(im)
        meta.append((bb, rows))

    nc = build_nc()
    res = run_bass_kernel_spmd(nc, in_maps, core_ids=list(range(8)))

    out = np.zeros((BB, S, D), np.float32)
    for c in range(8):
        bb, rows = meta[c]
        shard = res.results[c]["out_fm"]
        out[bb, rows[0:512], :] = shard[0:D, :].T
        out[bb, rows[512:1024], :] = shard[D:2 * D, :].T
    return out


# version 07 (final)
# speedup vs baseline: 1.7890x; 1.0304x over previous
import math
import os
import numpy as np
import ml_dtypes
from contextlib import ExitStack
import concourse.bass as bass
import concourse.mybir as mybir
from concourse.bass_core_rust import IndirectOffsetOnAxis
from concourse.bass_utils import run_bass_kernel_spmd

F32 = mybir.dt.float32
F16 = mybir.dt.float16
BF16 = mybir.dt.bfloat16
I32 = mybir.dt.int32
AF = mybir.ActivationFunctionType
OP = mybir.AluOpType
AX = mybir.AxisListType
NPBF = ml_dtypes.bfloat16

BB, S, D = 2, 4096, 4096
H, G, DH = 32, 8, 128
HID = 11008
HC = 86
NSL_LO, NSL_HI = 16, 32
NSLOT = 48
SCALE = 1.0 / math.sqrt(128.0)
SHIFT = 12.0
NEGB = -1e30
EPS = 1e-5
UP_N = [11, 11, 11, 11, 11, 11, 10, 10]
UP_PAD = 11
NF4 = [-1.0, -0.6961928009986877, -0.5250730514526367, -0.39491748809814453,
       -0.28444138169288635, -0.18477343022823334, -0.09105003625154495, 0.0,
       0.07958029955625534, 0.16093020141124725, 0.24611230194568634,
       0.33791524171829224, 0.44070982933044434, 0.5626170039176941,
       0.7229568362236023, 1.0]


def up_glob(r, local):
    return sum(UP_N[:r]) + local


ENGS = ("sync", "gpsimd", "vector", "scalar", "tensor")
ESEM = {"sync": (None, "LD"), "gpsimd": ("PO", "LG"), "vector": ("DV", None),
        "scalar": ("AC", "LA"), "tensor": ("PE", None)}
ALL_SEMS = ["LD", "LG", "LA", "DV", "AC", "PE", "PO", "CC", "OT"]
COMPUTE_SEMS = {"DV": "vector", "AC": "scalar", "PE": "tensor", "PO": "gpsimd"}


class Res:
    __slots__ = ("name", "prod", "readers", "init")

    def __init__(self, name, init=None):
        self.name = name
        self.prod = None
        self.readers = {}
        self.init = dict(init) if init else {}


class Bld:
    def __init__(self):
        self.q = {e: [] for e in ENGS}
        self.ctr = {s: 0 for s in ALL_SEMS}
        self.hw = {e: {s: 0 for s in ALL_SEMS} for e in ENGS}

    def step(self, eng, sem, inc, reads, writes, body):
        waits = {}

        def merge(tok):
            if tok is None:
                return
            s, c = tok
            if c > waits.get(s, 0):
                waits[s] = c

        for r in reads:
            merge(r.prod)
            for s, c in r.init.items():
                merge((s, c))
        for r in writes:
            merge(r.prod)
            for s, c in r.readers.items():
                merge((s, c))
            for s, c in r.init.items():
                merge((s, c))
        emit_w = {}
        for s, c in waits.items():
            if COMPUTE_SEMS.get(s) == eng:
                continue
            if c <= self.hw[eng][s]:
                continue
            emit_w[s] = c
            self.hw[eng][s] = c
        self.ctr[sem] += inc
        cnt = self.ctr[sem]
        self.q[eng].append((emit_w, body, sem, inc))
        tok = (sem, cnt)
        for r in reads:
            if cnt > r.readers.get(sem, 0):
                r.readers[sem] = cnt
        for r in writes:
            r.prod = tok
            r.readers = {}
            r.init = {}
        return tok

    def snapshot(self):
        return dict(self.ctr)


KTRUNC = int(os.environ.get("KTRUNC", "99"))
KT2 = int(os.environ.get("KT2", "99"))
KSW = os.environ.get("KSW", "")
KT6 = int(os.environ.get("KT6", "99"))
KD = os.environ.get("KD", "")
KDBG = os.environ.get("KDBG", "") == "1"


class _Done(Exception):
    pass


def build_nc():
    nc = bass.Bass("TRN2", target_bir_lowering=False)
    dt = nc.dram_tensor

    x_own = dt("x_own", [1024, D], F32, kind="ExternalInput").ap()
    x_ownT = dt("x_ownT", [D, 1024], F32, kind="ExternalInput").ap()
    Wq = dt("Wq", [D, D], F32, kind="ExternalInput").ap()
    Wk = dt("Wk", [D, 1024], F32, kind="ExternalInput").ap()
    Wv = dt("Wv", [D, 1024], F32, kind="ExternalInput").ap()
    Wo = dt("Wo", [D, D], F32, kind="ExternalInput").ap()
    bq_i = dt("bq_i", [128, 32], F32, kind="ExternalInput").ap()
    bk_i = dt("bk_i", [128, 8], F32, kind="ExternalInput").ap()
    bv_i = dt("bv_i", [1, 1024], F32, kind="ExternalInput").ap()
    bo_i = dt("bo_i", [128, 32], F32, kind="ExternalInput").ap()
    bd_i = dt("bd_i", [128, 32], F32, kind="ExternalInput").ap()
    bu_i = dt("bu_i", [128, HC], F32, kind="ExternalInput").ap()
    cosq = dt("cosq", [128, 1024], BF16, kind="ExternalInput").ap()
    sinq = dt("sinq", [128, 1024], BF16, kind="ExternalInput").ap()
    masks_i = dt("masks_i", [128, 4 * 512], BF16, kind="ExternalInput").ap()
    ident_i = dt("ident_i", [128, 128], BF16, kind="ExternalInput").ap()
    btbl_i = dt("btbl_i", [128, NSLOT], F32, kind="ExternalInput").ap()
    kidx_i = dt("kidx_i", [128, G * NSLOT], I32, kind="ExternalInput").ap()
    cod_u = dt("cod_u", [UP_PAD * 128, D], I32, kind="ExternalInput").ap()
    amx_u = dt("amx_u", [UP_PAD * 128, 64], F32, kind="ExternalInput").ap()
    cod_d = dt("cod_d", [512, HID], I32, kind="ExternalInput").ap()
    amx_d = dt("amx_d", [512, 172], F32, kind="ExternalInput").ap()
    la_u = dt("la_u", [D, 16], F32, kind="ExternalInput").ap()
    lb_u = dt("lb_u", [16, HID], F32, kind="ExternalInput").ap()
    la_d = dt("la_d", [HID, 16], F32, kind="ExternalInput").ap()
    lb_d = dt("lb_d", [16, D], F32, kind="ExternalInput").ap()
    out_fm = dt("out_fm", [2 * D, 512], F32, kind="ExternalOutput").ap()

    if KDBG:
        dbg_kg = dt("dbg_kg", [4 * G * 8 * 128, 128], BF16, kind="ExternalOutput").ap()
        dbg_vg = dt("dbg_vg", [4 * G * 8 * 128, 128], BF16, kind="ExternalOutput").ap()
        dbg_ctx = dt("dbg_ctx", [H * 128, 1024], BF16, kind="ExternalOutput").ap()
        dbg_atn = dt("dbg_atn", [D, 1024], F32, kind="ExternalOutput").ap()
        dbg_wu = dt("dbg_wu", [UP_PAD * 128, D], BF16, kind="ExternalOutput").ap()
        dbg_xn = dt("dbg_xn", [128, 32 * 512], BF16, kind="ExternalOutput").ap()
        dbg_x2 = dt("dbg_x2", [128, HC * 512], BF16, kind="ExternalOutput").ap()
        dbg_tl = dt("dbg_tl", [16, 1024], BF16, kind="ExternalOutput").ap()
        dbg_st = dt("dbg_st", [128, 8], F32, kind="ExternalOutput").ap()
        dbg_kps = dt("dbg_kps", [128, 512], F32, kind="ExternalOutput").ap()
    k_own = dt("k_own", [G * 8 * 128, 128], BF16)
    v_own = dt("v_own", [G * 8 * 128, 128], BF16)
    k_gth = dt("k_gth", [4 * G * 8 * 128, 128], BF16)
    v_gth = dt("v_gth", [4 * G * 8 * 128, 128], BF16)
    wu_own = dt("wu_own", [UP_PAD * 128, D], BF16)
    wd_own = dt("wd_own", [512, HID], BF16)
    wu_gth = dt("wu_gth", [8 * UP_PAD * 128, D], BF16)
    wd_gth = dt("wd_gth", [D, HID], BF16)
    ctx_dr = dt("ctx_dr", [H * 128, 1024], BF16)
    atn_dr = dt("atn_dr", [D, 1024], F32)

    b = Bld()

    with ExitStack() as ctx:
        sem_h = {s: ctx.enter_context(nc.semaphore(s)) for s in ALL_SEMS}
        HEAPB = 206 * 1024
        heap_t = ctx.enter_context(nc.sbuf_tensor("heap", [128, HEAPB // 2], BF16))
        heap = heap_t[:]
        cur = [0]

        def carve(nbytes, align=128):
            o = (cur[0] + align - 1) // align * align
            cur[0] = o + nbytes
            assert cur[0] <= HEAPB, f"heap overflow at {cur[0]}"
            return o

        def view(off, parts, n, dtyp):
            sz = {F32: 4, F16: 2, BF16: 2, I32: 4}[dtyp]
            v = heap[0:parts, off // 2: (off + n * sz) // 2]
            if dtyp != BF16:
                v = v.bitcast(dtyp)
            return v

        o_cos = carve(2048); o_sin = carve(2048)
        o_idn = carve(256); o_btb = carve(NSLOT * 4)
        o_kidx = carve(G * NSLOT * 4)
        o_bq = carve(128); o_bk = carve(32); o_bv = carve(2048)
        o_bo = carve(128); o_bd = carve(128); o_bu = carve(HC * 4)
        o_bf = carve(128)
        o_on1 = carve(4); o_on2 = carve(256); o_on3 = carve(512)
        o_eps = carve(128)
        o_lau = carve(32 * 16 * 2); o_lad = carve(HC * 16 * 2)
        o_st = carve(32)
        o_wgr = [carve(1024) for _ in range(8)]
        o_tl = carve(1024); o_t2 = carve(1024)
        o_lbu = [carve(1024) for _ in range(2)]
        o_lbd = [carve(1024) for _ in range(2)]
        o_xna = carve(32 * 1024)
        o_xnb = carve(32 * 1024)
        o_ph = carve(0)
        PH_SZ = HEAPB - o_ph
        assert PH_SZ >= 101000, f"phase area only {PH_SZ}"
        o_rt = [o_ph + 80 * 1024, o_ph + 82 * 1024]
        o_cev = o_ph + 84 * 1024
        o_qbf = [o_ph + 52 * 1024 + i * 2048 for i in range(4)]
        o_ptl = [o_ph + 60 * 1024 + i * 2048 for i in range(4)]
        o_rrp = o_ph + 68 * 1024
        o_sum = o_ph + 72 * 1024
        o_rin = o_ph + 76 * 1024
        o_fev = [o_ph + 92160 + i * 2048 for i in range(4)]

        cos_s = view(o_cos, 128, 1024, BF16)
        sin_s = view(o_sin, 128, 1024, BF16)
        idn_s = view(o_idn, 128, 128, BF16)
        btb_s = view(o_btb, 128, NSLOT, F32)
        kidx_s = view(o_kidx, 128, G * NSLOT, I32)
        bq_s = view(o_bq, 128, 32, F32)
        bk_s = view(o_bk, 128, 8, F32)
        bv_s = view(o_bv, 1, 1024, BF16)
        bo_s = view(o_bo, 128, 32, F32)
        bd_s = view(o_bd, 128, 32, F32)
        bu_s = view(o_bu, 128, HC, F32)
        bf_s = view(o_bf, 128, 32, F32)
        eps_s = view(o_eps, 128, 1, F32)
        ones_c = view(o_on1, 128, 1, BF16)
        ones_r = view(o_on2, 1, 128, BF16)
        onesf_r = view(o_on3, 1, 128, F32)
        lau_s = view(o_lau, 128, 32 * 16, BF16)
        lad_s = view(o_lad, 128, HC * 16, BF16)
        st_s = view(o_st, 128, 8, F32)
        wgr = [view(o, 128, 512, BF16) for o in o_wgr]
        rtmp = [view(o, 128, 512, F32) for o in o_rt]
        cev_s = view(o_cev, 128, 1024, BF16)
        fev = [view(o, 128, 512, F32) for o in o_fev]
        qbf = [view(o, 128, 1024, BF16) for o in o_qbf]
        ptil = [view(o, 128, 1024, BF16) for o in o_ptl]
        rrep_s = view(o_rrp, 128, 1024, F32)
        sums_s = view(o_sum, 1, 1024, F32)
        rinv_s = view(o_rin, 1, 1024, F32)
        tl_s = view(o_tl, 16, 512, BF16)
        t2_s = view(o_t2, 16, 512, BF16)
        lbur = [view(o, 16, 512, BF16) for o in o_lbu]
        lbdr = [view(o, 16, 512, BF16) for o in o_lbd]
        xn_a = view(o_xna, 128, 32 * 512, BF16)
        xn_b = view(o_xnb, 128, 32 * 512, BF16)

        xt_s = view(o_ph, 128, D, F32)
        xq_s = view(o_ph + 16384, 128, D, BF16)
        sq_s = view(o_ph + 24576, 128, D, F32)
        dq_off = [o_ph, o_ph + 24576 + 512]
        dqc = [view(dq_off[i], 128, 2048, I32) for i in range(2)]
        dq16 = [view(dq_off[i] + 8192, 128, 2048, F16) for i in range(2)]
        dqa = [view(dq_off[i] + 12288, 128, 2048, F16) for i in range(2)]
        dqt = [view(dq_off[i] + 16384, 128, 2048, F16) for i in range(2)]
        dqw = [view(dq_off[i] + 20480, 128, 2048, BF16) for i in range(2)]
        dqm = [view(dq_off[i] + 24576, 128, 64, F32) for i in range(2)]
        kst = [view(o_ph + i * 12288, 128, NSLOT * 128, BF16) for i in range(2)]
        vst = [view(o_ph + 24576 + i * 12288, 128, NSLOT * 128, BF16) for i in range(2)]
        msk_s = view(o_ph + 49152, 128, 4 * 512, BF16)
        ctv = [view(o_ph + 88064 + i * 1024, 128, 512, BF16) for i in range(4)]
        x2c = [view(o_ph + i * 1024, 128, 512, BF16) for i in range(HC)]
        assert o_ph + 92160 + 8192 <= HEAPB

        psA = ctx.enter_context(nc.psum_tensor("psA", [128, 1024], F32))
        psB = ctx.enter_context(nc.psum_tensor("psB", [128, 1024], F32))
        ps4 = ctx.enter_context(nc.psum_tensor("ps4", [128, 512], F32))
        ps5 = ctx.enter_context(nc.psum_tensor("ps5", [128, 512], F32))
        ps6 = ctx.enter_context(nc.psum_tensor("ps6", [128, 512], F32))
        ps7 = ctx.enter_context(nc.psum_tensor("ps7", [128, 512], F32))
        pbank = [psA.ap()[:, 0:512], psA.ap()[:, 512:1024],
                 psB.ap()[:, 0:512], psB.ap()[:, 512:1024],
                 ps4.ap()[:], ps5.ap()[:], ps6.ap()[:], ps7.ap()[:]]

        R = lambda n: Res(n)
        r_const = R("const")
        r_xt, r_xq, r_st = R("xt"), R("xq"), R("st")
        r_xna, r_xnb = R("xna"), R("xnb")
        r_wgr = [R(f"wgr{i}") for i in range(8)]
        r_rt0, r_rt1 = R("rt0"), R("rt1")
        r_cev = R("cev")
        r_fev = [R(f"fev{i}") for i in range(4)]
        r_qbf = [R(f"qbf{i}") for i in range(4)]
        r_ptl = [R(f"ptl{i}") for i in range(4)]
        r_rrp, r_sumv, r_rin = R("rrp"), R("sumv"), R("rin")
        r_tl, r_t2 = R("tl"), R("t2")
        r_lbur = [R(f"lbur{i}") for i in range(2)]
        r_lbdr = [R(f"lbdr{i}") for i in range(2)]
        r_dq = [R(f"dq{i}") for i in range(2)]
        r_dqc = [R(f"dqc{i}") for i in range(2)]
        r_dqm = [R(f"dqm{i}") for i in range(2)]
        r_kst = [R(f"kst{i}") for i in range(2)]
        r_vst = [R(f"vst{i}") for i in range(2)]
        r_msk = R("msk")
        r_ctv = [R(f"ctv{i}") for i in range(4)]
        r_x2 = [R(f"x2_{i}") for i in range(HC)]
        r_ps = [R(f"ps{i}") for i in range(8)]
        r_psA, r_psB = r_ps[0], r_ps[2]
        r_kown, r_vown, r_kg, r_vg = R("kown"), R("vown"), R("kg"), R("vg")
        r_wuo, r_wdo, r_wug, r_wdg = R("wuo"), R("wdo"), R("wug"), R("wdg")
        r_ctxd, r_atnd = R("ctxd"), R("atnd")

        def bres(i):
            if i < 2:
                return r_ps[0]
            if i < 4:
                return r_ps[2]
            return r_ps[i]

        LDs, LGs, LAs, DVs, ACs, PEs, POs, CCs, OTs = (sem_h[s] for s in ALL_SEMS)
        SH = sem_h

        def wrap(body, emit_w, sem, inc):
            def fn(e):
                for s, c in emit_w.items():
                    e.wait_ge(SH[s], c)
                body(e)
            return fn

        def dstep(eng, reads, writes, mk, n_dma=1):
            sem = ESEM[eng][1]

            def body(e, mk=mk, sem=sem):
                mk(e, lambda inst: inst.then_inc(SH[sem], 16))
            return b.step(eng, sem, 16 * n_dma, reads, writes, body)

        def cstep(eng, reads, writes, mk):
            sem = ESEM[eng][0]

            def body(e, mk=mk, sem=sem):
                last = mk(e)
                last.then_inc(SH[sem], 1)
            return b.step(eng, sem, 1, reads, writes, body)


        class _G:
            pass
        try:
            def ld1(dst, src):
                return dstep("sync", [], [r_const],
                             lambda e, inc, d=dst, s=src: inc(e.dma_start(out=d, in_=s)))
            ld1(cos_s, cosq); ld1(sin_s, sinq); ld1(idn_s, ident_i)
            ld1(btb_s, btbl_i); ld1(kidx_s, kidx_i)
            ld1(bq_s, bq_i); ld1(bk_s, bk_i); ld1(bo_s, bo_i); ld1(bd_s, bd_i)
            ld1(bu_s, bu_i)
            dstep("gpsimd", [], [r_const],
                  lambda e, inc: inc(e.dma_start(out=bv_s, in_=bv_i)))
            dstep("gpsimd", [], [r_const],
                  lambda e, inc: inc(e.dma_start(
                      out=lau_s.rearrange("p (c j) -> p c j", c=32),
                      in_=la_u.rearrange("(c p) j -> p c j", p=128))))
            dstep("gpsimd", [], [r_const],
                  lambda e, inc: inc(e.dma_start(
                      out=lad_s.rearrange("p (c j) -> p c j", c=HC),
                      in_=la_d.rearrange("(c p) j -> p c j", p=128))))
    
            def _ones(v):
                v.memset(ones_c, 1.0)
                v.memset(ones_r, 1.0)
                v.memset(onesf_r, 1.0)
                v.memset(eps_s, EPS)
                return v.tensor_add(out=bf_s, in0=bo_s, in1=bd_s)
            cstep("vector", [r_const], [r_const], _ones)
    
            if KTRUNC < 1:
                raise _Done
            for i in range(8):
                xnr = r_xna if i < 4 else r_xnb
                xnv = xn_a if i < 4 else xn_b
                toff = (i % 4) * 128
                dstep("sync", [], [r_xt],
                      lambda e, inc, i=i: inc(e.dma_start(out=xt_s, in_=x_own[i * 128:(i + 1) * 128, :])))
    
                cstep("vector", [r_xt], [r_st],
                      lambda v: v.reduce_sum(out=st_s[:, 0:1], in_=xt_s, axis=AX.X))
                cstep("gpsimd", [r_st], [r_st],
                      lambda g: g.tensor_scalar(out=st_s[:, 1:2], in0=st_s[:, 0:1],
                                                scalar1=1.0 / D, scalar2=None, op0=OP.mult))
    
                def _sq(v):
                    v.tensor_tensor(out=sq_s, in0=xt_s, in1=xt_s, op=OP.mult)
                    return v.reduce_sum(out=st_s[:, 2:3], in_=sq_s, axis=AX.X)
                cstep("vector", [r_xt, r_st], [r_st], _sq)
    
                cstep("gpsimd", [r_st], [r_st],
                      lambda g: g.tensor_scalar(out=st_s[:, 3:4], in0=st_s[:, 2:3],
                                                scalar1=1.0 / D, scalar2=None, op0=OP.mult))
                cstep("vector", [r_st], [r_st],
                      lambda v: v.scalar_tensor_tensor(out=st_s[:, 4:5], in0=st_s[:, 1:2],
                                                       scalar=st_s[:, 1:2], in1=st_s[:, 3:4],
                                                       op0=OP.mult, op1=OP.subtract))
    
                cstep("scalar", [r_st], [r_st],
                      lambda sc: sc.activation(out=st_s[:, 5:6], in_=st_s[:, 4:5],
                                               func=AF.Ln, bias=eps_s, scale=-1.0))
                cstep("vector", [r_st], [r_st],
                      lambda v: v.tensor_copy(out=st_s[:, 7:8], in_=st_s[:, 5:6]))
                cstep("scalar", [r_st], [r_st],
                      lambda sc: sc.activation(out=st_s[:, 6:7], in_=st_s[:, 7:8],
                                               func=AF.Exp, scale=-0.5))
    
                def _norm(v):
                    v.tensor_scalar(out=sq_s, in0=xt_s, scalar1=st_s[:, 1:2],
                                    scalar2=None, op0=OP.subtract)
                    return v.tensor_scalar(out=xq_s, in0=sq_s, scalar1=st_s[:, 6:7],
                                           scalar2=None, op0=OP.mult)
                cstep("vector", [r_xt, r_st], [r_xq], _norm)
    
                for j in range(32):
                    bank = r_ps[6] if j % 2 == 0 else r_ps[7]
                    pv = (ps6 if j % 2 == 0 else ps7).ap().bitcast(BF16)[:, 0:128]
                    cstep("tensor", [r_xq, r_const], [bank],
                          lambda t, j=j, pv=pv: t.transpose(pv, xq_s[:, j * 128:(j + 1) * 128], idn_s))
                    cstep("vector", [bank], [xnr],
                          lambda v, j=j, pv=pv, xnv=xnv, toff=toff:
                          v.tensor_copy(out=xnv[:, j * 512 + toff: j * 512 + toff + 128], in_=pv))
    
            if KTRUNC < 2:
                raise _Done
            ring = [0]
    
            def next_ring():
                i = ring[0] % 8
                ring[0] += 1
                return i
    
            def xslice(d, th):
                return (xn_a if th == 0 else xn_b)[:, d * 512:(d + 1) * 512]
    
            for gq in range(2):
                grans = []
                for d in range(32):
                    i = next_ring()
                    dstep("gpsimd", [], [r_wgr[i]],
                          lambda e, inc, d=d, gq=gq, i=i: inc(e.dma_start(
                              out=wgr[i], in_=Wk[d * 128:(d + 1) * 128, gq * 512:(gq + 1) * 512])))
                    def _mmk(t, i=i, d=d):
                        last = None
                        for sub in range(4):
                            for th in range(2):
                                last = t.matmul(pbank[sub * 2 + th], wgr[i][:, sub * 128:(sub + 1) * 128],
                                                xslice(d, th), start=(d == 0), stop=(d == 31))
                        return last
                    cstep("tensor", [r_wgr[i], r_xna, r_xnb],
                          [bres(k) for k in range(8)], _mmk)
                if KDBG and gq == 1:
                    cstep("vector", [r_ps[0]], [r_fev[0]],
                          lambda v: v.tensor_copy(out=fev[0], in_=pbank[0]))
                    dstep("sync", [r_fev[0]], [],
                          lambda e, inc: inc(e.dma_start(out=dbg_kps, in_=fev[0])))
                for sub in range(4):
                    g = gq * 4 + sub
                    for th in range(2):
                        bi = sub * 2 + th
                        cs = cos_s[:, th * 512:(th + 1) * 512]
                        ss = sin_s[:, th * 512:(th + 1) * 512]
    
                        cstep("vector", [bres(bi), r_const], [r_rt0],
                              lambda v, bi=bi, g=g: v.tensor_scalar(
                                  out=rtmp[0], in0=pbank[bi], scalar1=bk_s[:, g:g + 1],
                                  scalar2=None, op0=OP.add))
    
                        if "w" not in KSW:
                            def _swp(e, inc):
                                inc(e.dma_start(out=rtmp[1][0:64, :], in_=rtmp[0][64:128, :]))
                                inc(e.dma_start(out=rtmp[1][64:128, :], in_=rtmp[0][0:64, :]))
                            dstep("sync", [r_rt0], [r_rt1], _swp, n_dma=2)

                        if "r" not in KSW:
                            def _ropek(v, cs=cs, ss=ss):
                                v.tensor_tensor(out=rtmp[0], in0=rtmp[0], in1=cs, op=OP.mult)
                                v.tensor_tensor(out=rtmp[1], in0=rtmp[1], in1=ss, op=OP.mult)
                                return v.tensor_add(out=cev_s[:, 0:512], in0=rtmp[0], in1=rtmp[1])
                            cstep("vector", [r_rt0, r_rt1, r_const], [r_rt0, r_rt1, r_cev], _ropek)
                        else:
                            cstep("vector", [r_rt0], [r_cev],
                                  lambda v: v.tensor_copy(out=cev_s[:, 0:512], in_=rtmp[0]))
                        dst = k_own.ap().rearrange("(g bl kd) u -> g kd bl u", g=G, bl=8)[g, :, th * 4:th * 4 + 4, :]
                        src = cev_s[:, 0:512].rearrange("kd (bl u) -> kd bl u", bl=4)
                        if "s" not in KSW:
                            dstep("sync", [r_cev], [r_kown],
                                  lambda e, inc, dst=dst, src=src: inc(
                                      e.dma_start(out=dst, in_=src)))
    
            if KT2 < 1:
                raise _Done
            for vh in range(2):
                grans = []
                for d in range(32):
                    i = next_ring()
                    dstep("gpsimd", [], [r_wgr[i]],
                          lambda e, inc, d=d, vh=vh, i=i: inc(e.dma_start(
                              out=wgr[i], in_=Wv[d * 128:(d + 1) * 128, vh * 512:(vh + 1) * 512])))
                    def _mmv(t, i=i, d=d):
                        last = None
                        for tc in range(8):
                            xnh = xn_a if tc < 4 else xn_b
                            toff = (tc % 4) * 128
                            last = t.matmul(pbank[tc], xnh[:, d * 512 + toff: d * 512 + toff + 128],
                                            wgr[i], start=(d == 0), stop=False)
                        return last
                    cstep("tensor", [r_wgr[i], r_xna, r_xnb],
                          [bres(k) for k in range(8)], _mmv)
    
                def _mvb(t, vh=vh):
                    last = None
                    for tc in range(8):
                        last = t.matmul(pbank[tc], ones_r, bv_s[:, vh * 512:(vh + 1) * 512],
                                        start=False, stop=True)
                    return last
                cstep("tensor", [r_const], [bres(k) for k in range(8)], _mvb)
                for tc in range(8):
                    cstep("vector", [bres(tc)], [r_cev],
                          lambda v, tc=tc: v.tensor_copy(out=cev_s[:, 0:512], in_=pbank[tc]))
                    dst = v_own.ap().rearrange("(g bl u) k -> bl u g k", g=G, bl=8)[tc, :, vh * 4:vh * 4 + 4, :]
                    src = cev_s[:, 0:512].rearrange("u (g k) -> u g k", g=4)
                    dstep("sync", [r_cev], [r_vown],
                          lambda e, inc, dst=dst, src=src: inc(e.dma_start(out=dst, in_=src)))
    
            if KT2 < 2:
                raise _Done
            def _cck(e):
                e.collective_compute("AllGather", OP.bypass,
                                     replica_groups=[[0, 1, 2, 3], [4, 5, 6, 7]],
                                     ins=[k_own.ap().opt()], outs=[k_gth.ap().opt()]).then_inc(SH["CC"], 1)
            b.step("gpsimd", "CC", 1, [r_kown], [r_kg], _cck)
    
            def _ccv(e):
                e.collective_compute("AllGather", OP.bypass,
                                     replica_groups=[[0, 1, 2, 3], [4, 5, 6, 7]],
                                     ins=[v_own.ap().opt()], outs=[v_gth.ap().opt()]).then_inc(SH["CC"], 1)
            b.step("gpsimd", "CC", 1, [r_vown], [r_vg], _ccv)
    
            if KTRUNC < 3:
                raise _Done
            gate_p3 = b.snapshot()
            for rr in (r_dq + r_dqc + r_dqm):
                rr.init = dict(gate_p3)
            units = []
            for hcl in range(UP_PAD):
                for sub2 in range(2):
                    units.append(("u", hcl, sub2))
            for dr in range(4):
                for sub2 in range(6):
                    units.append(("d", dr, sub2))
            for ui, (kind, a, sub2) in enumerate(units):
                en = ui % 2
                if kind == "u":
                    w = 2048
                    csrc = cod_u[a * 128:(a + 1) * 128, sub2 * 2048:(sub2 + 1) * 2048]
                    msrc = amx_u[a * 128:(a + 1) * 128, sub2 * 32:(sub2 + 1) * 32]
                    nblk = 32
                    wdst = wu_own.ap()[a * 128:(a + 1) * 128, sub2 * 2048:(sub2 + 1) * 2048]
                    wres = r_wuo
                else:
                    w = 2048 if sub2 < 5 else 768
                    csrc = cod_d[a * 128:(a + 1) * 128, sub2 * 2048: sub2 * 2048 + w]
                    msrc = amx_d[a * 128:(a + 1) * 128, sub2 * 32: sub2 * 32 + w // 64]
                    nblk = w // 64
                    wdst = wd_own.ap()[a * 128:(a + 1) * 128, sub2 * 2048: sub2 * 2048 + w]
                    wres = r_wdo
                eng = "vector" if w == 2048 else "gpsimd"
                dstep("sync", [], [r_dqc[en]],
                      lambda e, inc, en=en, w=w, csrc=csrc: inc(
                          e.dma_start(out=dqc[en][:, 0:w], in_=csrc)))
                dstep("sync", [], [r_dqm[en]],
                      lambda e, inc, en=en, nblk=nblk, msrc=msrc: inc(
                          e.dma_start(out=dqm[en][:, 0:nblk], in_=msrc)))
    
                def _dq(e, en=en, w=w, nblk=nblk):
                    e.tensor_copy(out=dq16[en][:, 0:w], in_=dqc[en][:, 0:w])
                    e.memset(dqa[en][:, 0:w], 0.0)
                    for kk in range(16):
                        e.tensor_scalar(out=dqt[en][:, 0:w], in0=dq16[en][:, 0:w],
                                        scalar1=float(kk), scalar2=float(NF4[kk]),
                                        op0=OP.is_equal, op1=OP.mult)
                        e.tensor_tensor(out=dqa[en][:, 0:w], in0=dqa[en][:, 0:w],
                                        in1=dqt[en][:, 0:w], op=OP.add)
                    return e.tensor_tensor(
                        out=dqw[en][:, 0:w].rearrange("p (a c) -> p a c", c=64),
                        in0=dqa[en][:, 0:w].rearrange("p (a c) -> p a c", c=64),
                        in1=dqm[en][:, 0:nblk].unsqueeze(2).to_broadcast([128, nblk, 64]),
                        op=OP.mult)
                cstep(eng, [r_dqc[en], r_dqm[en]], [r_dq[en]], _dq)
                dstep("scalar", [r_dq[en]], [wres],
                      lambda e, inc, en=en, w=w, wdst=wdst: inc(
                          e.dma_start(out=wdst, in_=dqw[en][:, 0:w])))
    
    
            if KTRUNC < 4:
                raise _Done
            gate_p4 = b.snapshot()
            for rr in (r_kst + r_vst + [r_msk] + r_qbf + r_ptl + [r_rrp, r_sumv, r_rin]):
                rr.init = dict(gate_p4)
            dstep("sync", [], [r_msk],
                  lambda e, inc: inc(e.dma_start(out=msk_s, in_=masks_i)))
    
            for g in range(G):
                sb = g % 2
                for s in range(NSLOT):
                    idx = kidx_s[:, g * NSLOT + s: g * NSLOT + s + 1]
    
                    def _stg(e, inc, s=s, sb=sb, idx=idx):
                        inc(e.indirect_dma_start(kst[sb][:, s * 128:(s + 1) * 128], None,
                                                 k_gth.ap(), IndirectOffsetOnAxis(idx, 0)))
                        inc(e.indirect_dma_start(vst[sb][:, s * 128:(s + 1) * 128], None,
                                                 v_gth.ap(), IndirectOffsetOnAxis(idx, 0)))
                    dstep("gpsimd", [r_kg, r_vg, r_const], [r_kst[sb], r_vst[sb]], _stg, n_dma=2)
    
                grans = []
                for d in range(32):
                    i = next_ring()
                    dstep("gpsimd", [], [r_wgr[i]],
                          lambda e, inc, d=d, g=g, i=i: inc(e.dma_start(
                              out=wgr[i], in_=Wq[d * 128:(d + 1) * 128, g * 512:(g + 1) * 512])))
                    def _mmq(t, i=i, d=d):
                        last = None
                        for rh in range(4):
                            for th in range(2):
                                last = t.matmul(pbank[rh * 2 + th], wgr[i][:, rh * 128:(rh + 1) * 128],
                                                xslice(d, th), start=(d == 0), stop=(d == 31))
                        return last
                    cstep("tensor", [r_wgr[i], r_xna, r_xnb],
                          [bres(k) for k in range(8)], _mmq)
                for rh in range(4):
                    h = g * 4 + rh
                    for th in range(2):
                        bi = rh * 2 + th
                        cs = cos_s[:, th * 512:(th + 1) * 512]
                        ss = sin_s[:, th * 512:(th + 1) * 512]
    
                        cstep("vector", [bres(bi), r_const], [r_rt0],
                              lambda v, bi=bi, h=h: v.tensor_scalar(
                                  out=rtmp[0], in0=pbank[bi], scalar1=bq_s[:, h:h + 1],
                                  scalar2=None, op0=OP.add))
    
                        def _swpq(e, inc):
                            inc(e.dma_start(out=rtmp[1][0:64, :], in_=rtmp[0][64:128, :]))
                            inc(e.dma_start(out=rtmp[1][64:128, :], in_=rtmp[0][0:64, :]))
                        dstep("sync", [r_rt0], [r_rt1], _swpq, n_dma=2)
    
                        def _ropeq(v, cs=cs, ss=ss, rh=rh, th=th):
                            v.tensor_tensor(out=rtmp[0], in0=rtmp[0], in1=cs, op=OP.mult)
                            v.tensor_tensor(out=rtmp[1], in0=rtmp[1], in1=ss, op=OP.mult)
                            return v.tensor_add(out=qbf[rh][:, th * 512:(th + 1) * 512],
                                                in0=rtmp[0], in1=rtmp[1])
                        cstep("vector", [r_rt0, r_rt1, r_const], [r_rt0, r_rt1, r_qbf[rh]], _ropeq)
    
                for rh in range(4):
                    h = g * 4 + rh
                    for ch in range(2):
                        base = 0 if ch == 0 else NSL_LO
                        npair = (NSL_LO if ch == 0 else NSL_HI) // 2
                        ctx_r = r_ps[4] if ch == 0 else r_ps[5]
                        ctx_b = ps4 if ch == 0 else ps5
                        for pr in range(npair):
                            s0 = base + 2 * pr
                            sc_r = r_psA if pr % 2 == 0 else r_psB
                            sc_b = psA if pr % 2 == 0 else psB
    
                            def _mm1(t, s0=s0, sc_b=sc_b, rh=rh, ch=ch, sb=sb):
                                last = None
                                for k in range(2):
                                    last = t.matmul(sc_b.ap()[:, k * 512:(k + 1) * 512],
                                                    kst[sb][:, (s0 + k) * 128:(s0 + k + 1) * 128],
                                                    qbf[rh][:, ch * 512:(ch + 1) * 512],
                                                    start=True, stop=True)
                                return last
                            cstep("tensor", [r_kst[sb], r_qbf[rh]], [sc_r], _mm1)
    
                            pi = (ch * 16 + pr) % 4
    
                            def _exp(sc, sc_b=sc_b, pi=pi, s0=s0):
                                return sc.activation(out=ptil[pi], in_=sc_b.ap(),
                                                     func=AF.Exp, bias=btb_s[:, s0:s0 + 1],
                                                     scale=SCALE)
                            cstep("scalar", [sc_r, r_const], [r_ptl[pi]], _exp)
    
                            if pr < 2:
                                def _msk2(v, pi=pi, pr=pr):
                                    v.tensor_tensor(out=ptil[pi][:, 0:512],
                                                    in0=ptil[pi][:, 0:512],
                                                    in1=msk_s[:, 2 * pr * 512:(2 * pr + 1) * 512],
                                                    op=OP.mult)
                                    return v.tensor_tensor(out=ptil[pi][:, 512:1024],
                                                           in0=ptil[pi][:, 512:1024],
                                                           in1=msk_s[:, (2 * pr + 1) * 512:(2 * pr + 2) * 512],
                                                           op=OP.mult)
                                cstep("vector", [r_msk], [r_ptl[pi]], _msk2)
    
                            def _mm2(t, pi=pi, s0=s0, pr=pr, npair=npair, ctx_b=ctx_b, sb=sb):
                                for k in range(2):
                                    t.matmul(ps6.ap()[0:1, 0:512], ones_c,
                                             ptil[pi][:, k * 512:(k + 1) * 512],
                                             start=(pr == 0 and k == 0),
                                             stop=(pr == npair - 1 and k == 1))
                                last = None
                                for k in range(2):
                                    last = t.matmul(ctx_b.ap(),
                                                    vst[sb][:, (s0 + k) * 128:(s0 + k + 1) * 128],
                                                    ptil[pi][:, k * 512:(k + 1) * 512],
                                                    start=(pr == 0 and k == 0),
                                                    stop=(pr == npair - 1 and k == 1))
                                return last
                            cstep("tensor", [r_ptl[pi], r_vst[sb], r_const],
                                  [r_ps[6], ctx_r], _mm2)
                        cstep("scalar", [r_ps[6]], [r_sumv],
                              lambda sc, ch=ch: sc.copy(out=sums_s[:, ch * 512:(ch + 1) * 512],
                                                        in_=ps6.ap()[0:1, 0:512]))
                        cstep("vector", [r_sumv], [r_rin],
                              lambda v, ch=ch: v.reciprocal(out=rinv_s[:, ch * 512:(ch + 1) * 512],
                                                            in_=sums_s[:, ch * 512:(ch + 1) * 512]))
                    for ch in range(2):
                        cstep("tensor", [r_rin, r_const], [r_ps[7]],
                              lambda t, ch=ch: t.matmul(ps7.ap(), onesf_r,
                                                        rinv_s[:, ch * 512:(ch + 1) * 512],
                                                        start=True, stop=True))
                        cstep("vector", [r_ps[7]], [r_rrp],
                              lambda v, ch=ch: v.tensor_copy(out=rrep_s[:, ch * 512:(ch + 1) * 512],
                                                             in_=ps7.ap()))
    
                    def _cev2(v):
                        v.tensor_tensor(out=cev_s[:, 0:512], in0=ps4.ap(),
                                        in1=rrep_s[:, 0:512], op=OP.mult)
                        return v.tensor_tensor(out=cev_s[:, 512:1024], in0=ps5.ap(),
                                               in1=rrep_s[:, 512:1024], op=OP.mult)
                    cstep("vector", [r_ps[4], r_ps[5], r_rrp], [r_cev], _cev2)
                    dstep("sync", [r_cev], [r_ctxd],
                          lambda e, inc, h=h: inc(e.dma_start(
                              out=ctx_dr.ap()[h * 128:(h + 1) * 128, :], in_=cev_s)))
    
            if KTRUNC < 5:
                raise _Done
            def emit_w_gathers():
                def _ccwu(e):
                    e.collective_compute("AllGather", OP.bypass,
                                         replica_groups=[list(range(8))],
                                         ins=[wu_own.ap().opt()], outs=[wu_gth.ap().opt()]).then_inc(SH["CC"], 1)
                b.step("gpsimd", "CC", 1, [r_wuo], [r_wug], _ccwu)
    
                def _ccwd(e):
                    e.collective_compute("AllGather", OP.bypass,
                                         replica_groups=[list(range(8))],
                                         ins=[wd_own.ap().opt()], outs=[wd_gth.ap().opt()]).then_inc(SH["CC"], 1)
                b.step("gpsimd", "CC", 1, [r_wdo], [r_wdg], _ccwd)

            emit_w_gathers()
            gate_p5 = b.snapshot()
            for rr in r_ctv:
                rr.init = dict(gate_p5)
            for dq_ in range(8):
                grans = []
                for cd in range(32):
                    i = next_ring()
                    dstep("gpsimd", [], [r_wgr[i]],
                          lambda e, inc, cd=cd, dq_=dq_, i=i: inc(e.dma_start(
                              out=wgr[i], in_=Wo[cd * 128:(cd + 1) * 128, dq_ * 512:(dq_ + 1) * 512])))
                    ci = cd % 2
    
                    def _ctl(e, inc, cd=cd, ci=ci):
                        inc(e.dma_start(out=ctv[2 * ci], in_=ctx_dr.ap()[cd * 128:(cd + 1) * 128, 0:512]))
                        inc(e.dma_start(out=ctv[2 * ci + 1], in_=ctx_dr.ap()[cd * 128:(cd + 1) * 128, 512:1024]))
                    dstep("sync", [r_ctxd], [r_ctv[2 * ci], r_ctv[2 * ci + 1]], _ctl, n_dma=2)

                    def _mmo(t, i=i, cd=cd, ci=ci):
                        last = None
                        for sub in range(4):
                            for th in range(2):
                                last = t.matmul(pbank[sub * 2 + th], wgr[i][:, sub * 128:(sub + 1) * 128],
                                                ctv[2 * ci + th], start=(cd == 0), stop=(cd == 31))
                        return last
                    cstep("tensor", [r_wgr[i], r_ctv[2 * ci], r_ctv[2 * ci + 1]],
                          [bres(k) for k in range(8)], _mmo)
                for sub in range(4):
                    dout = dq_ * 4 + sub
                    for th in range(2):
                        fi = 2 * (sub % 2) + th
                        cstep("vector", [bres(sub * 2 + th)], [r_fev[fi]],
                              lambda v, sub=sub, th=th, fi=fi: v.tensor_copy(
                                  out=fev[fi], in_=pbank[sub * 2 + th]))
                        dstep("sync", [r_fev[fi]], [r_atnd],
                              lambda e, inc, dout=dout, th=th, fi=fi: inc(e.dma_start(
                                  out=atn_dr.ap()[dout * 128:(dout + 1) * 128,
                                                  th * 512:(th + 1) * 512], in_=fev[fi])))
    
            if KTRUNC < 6:
                raise _Done
            gate_p6 = b.snapshot()
            for rr in r_x2:
                rr.init = dict(gate_p6)
            out_sem_total = [0]
            for th in range(2):
                def _tlmm(t, th=th):
                    last = None
                    for d in range(32):
                        last = t.matmul(ps6.ap()[0:16, 0:512], lau_s[:, d * 16:(d + 1) * 16],
                                        xslice(d, th), start=(d == 0), stop=(d == 31))
                    return last
                cstep("tensor", [r_const, r_xna, r_xnb], [r_ps[6]], _tlmm)
                cstep("vector", [r_ps[6]], [r_tl],
                      lambda v: v.tensor_copy(out=tl_s, in_=ps6.ap()[0:16, 0:512]))
    
                if KT6 < 1:
                    raise _Done
                hquads = []
                for r in range(8):
                    qsz = [4, 4, UP_N[r] - 8]
                    base = 0
                    for qs in qsz:
                        hquads.append((r, base, qs))
                        base += qs
                brot = [0]
                for (r, hbase, qs) in hquads:
                    li = (hbase // 4 + r) % 2
                    hg0 = up_glob(r, hbase)
                    dstep("gpsimd", [], [r_lbur[li]],
                          lambda e, inc, hg0=hg0, qs=qs, li=li: inc(e.dma_start(
                              out=lbur[li][:, 0:qs * 128],
                              in_=lb_u[:, hg0 * 128:(hg0 + qs) * 128])))
                    banks = [(brot[0] + j) % 8 for j in range(qs)]
                    for d in range(32):
                        i = next_ring()
                        src = wu_gth.ap()[r * UP_PAD * 128 + hbase * 128:
                                          r * UP_PAD * 128 + (hbase + qs) * 128,
                                          d * 128:(d + 1) * 128]
                        dstep("sync", [r_wug], [r_wgr[i]],
                              lambda e, inc, src=src, i=i, qs=qs: inc(
                                  e.dma_start_transpose(wgr[i][:, 0:qs * 128], src)))

                        def _mmu(t, i=i, d=d, qs=qs, banks=banks, th=th):
                            last = None
                            for j in range(qs):
                                last = t.matmul(pbank[banks[j]], wgr[i][:, j * 128:(j + 1) * 128],
                                                xslice(d, th), start=(d == 0), stop=False)
                            return last
                        cstep("tensor", [r_wgr[i], r_xna, r_xnb],
                              list({bres(x) for x in banks}), _mmu)
    
                    def _mlu(t, qs=qs, banks=banks, li=li):
                        last = None
                        for j in range(qs):
                            last = t.matmul(pbank[banks[j]], lbur[li][:, j * 128:(j + 1) * 128],
                                            tl_s, start=False, stop=True)
                        return last
                    cstep("tensor", [r_tl, r_lbur[li]], list({bres(x) for x in banks}), _mlu)
                    for j in range(qs):
                        hg = up_glob(r, hbase + j)
                        cstep("scalar", [bres(banks[j]), r_const], [r_x2[hg]],
                              lambda sc, hg=hg, bj=banks[j]: sc.activation(
                                  out=x2c[hg], in_=pbank[bj], func=AF.Relu,
                                  bias=bu_s[:, hg:hg + 1]))
                    brot[0] = (brot[0] + qs) % 8
    
                if KT6 < 2:
                    raise _Done
                def _t2mm(t):
                    last = None
                    for hc_ in range(HC):
                        last = t.matmul(ps6.ap()[0:16, 0:512], lad_s[:, hc_ * 16:(hc_ + 1) * 16],
                                        x2c[hc_], start=(hc_ == 0), stop=(hc_ == HC - 1))
                    return last
                cstep("tensor", [r_const] + r_x2, [r_ps[6]], _t2mm)
                cstep("vector", [r_ps[6]], [r_t2],
                      lambda v: v.tensor_copy(out=t2_s, in_=ps6.ap()[0:16, 0:512]))
    
                if KT6 < 3:
                    raise _Done
                for r in range(8):
                    banks = [(4 * (r % 2)) + j for j in range(4)]
                    li = r % 2
                    dstep("gpsimd", [], [r_lbdr[li]],
                          lambda e, inc, r=r, li=li: inc(e.dma_start(
                              out=lbdr[li][:, 0:512], in_=lb_d[:, r * 512:(r + 1) * 512])))
                    for hc_ in range(HC):
                        i = next_ring()
                        src = wd_gth.ap()[r * 512:(r + 1) * 512, hc_ * 128:(hc_ + 1) * 128]
                        dstep("scalar", [r_wdg], [r_wgr[i]],
                              lambda e, inc, src=src, i=i: inc(e.dma_start_transpose(wgr[i], src)))

                        def _mmd(t, i=i, hc_=hc_, banks=banks):
                            last = None
                            for j in range(4):
                                last = t.matmul(pbank[banks[j]], wgr[i][:, j * 128:(j + 1) * 128],
                                                x2c[hc_], start=(hc_ == 0), stop=False)
                            return last
                        cstep("tensor", [r_wgr[i], r_x2[hc_]],
                              list({bres(x) for x in banks}), _mmd)
    
                    def _mld(t, r=r, banks=banks, li=li):
                        last = None
                        for j in range(4):
                            last = t.matmul(pbank[banks[j]], lbdr[li][:, j * 128:(j + 1) * 128],
                                            t2_s, start=False, stop=True)
                        return last
                    cstep("tensor", [r_t2, r_lbdr[li]], list({bres(x) for x in banks}), _mld)
                    for j in range(4):
                        if "f" in KD:
                            continue
                        dout = r * 4 + j
                        fa = 2 * (j % 2)
    
                        def _fl(e, inc, dout=dout, th=th, fa=fa):
                            inc(e.dma_start(out=fev[fa],
                                            in_=atn_dr.ap()[dout * 128:(dout + 1) * 128,
                                                            th * 512:(th + 1) * 512]))
                            inc(e.dma_start(out=fev[fa + 1],
                                            in_=x_ownT[dout * 128:(dout + 1) * 128,
                                                       th * 512:(th + 1) * 512]))
                        dstep("sync", [r_atnd], [r_fev[fa], r_fev[fa + 1]], _fl, n_dma=2)
    
                        if "v" not in KD:
                            def _fin(v, dout=dout, fa=fa, bj=banks[j]):
                                v.scalar_tensor_tensor(out=fev[fa], in0=pbank[bj],
                                                       scalar=bf_s[:, dout:dout + 1],
                                                       in1=fev[fa], op0=OP.add, op1=OP.add)
                                return v.tensor_add(out=fev[fa], in0=fev[fa], in1=fev[fa + 1])
                            cstep("vector", [bres(banks[j]), r_fev[fa + 1], r_const],
                                  [r_fev[fa]], _fin)
                        else:
                            cstep("vector", [bres(banks[j]), r_fev[fa + 1], r_const],
                                  [r_fev[fa]],
                                  lambda v, fa=fa: v.tensor_copy(out=fev[fa], in_=fev[fa + 1]))
    
                        def _fst(e, inc, dout=dout, th=th, fa=fa):
                            inc(e.dma_start(out=out_fm[dout * 128:(dout + 1) * 128,
                                                       th * 512:(th + 1) * 512],
                                            in_=fev[fa]))
                        b.step("sync", "OT", 16, [r_fev[fa]], [],
                               lambda e, fa=fa, dout=dout, th=th: e.dma_start(
                                   out=out_fm[th * D + dout * 128: th * D + (dout + 1) * 128, :],
                                   in_=fev[fa]).then_inc(SH["OT"], 16))
                        out_sem_total[0] += 16
    
            if KDBG:
                def _dbg(e, inc):
                    inc(e.dma_start(out=dbg_kg, in_=k_gth.ap()))
                    inc(e.dma_start(out=dbg_vg, in_=v_gth.ap()))
                    inc(e.dma_start(out=dbg_ctx, in_=ctx_dr.ap()))
                    inc(e.dma_start(out=dbg_atn, in_=atn_dr.ap()))
                    inc(e.dma_start(out=dbg_wu, in_=wu_own.ap()))
                dstep("sync", [r_kg, r_vg, r_ctxd, r_atnd, r_wuo], [], _dbg, n_dma=5)
                dstep("sync", [r_xna], [],
                      lambda e, inc: inc(e.dma_start(out=dbg_xn, in_=xn_a)))
                dstep("sync", [r_st], [],
                      lambda e, inc: inc(e.dma_start(out=dbg_st, in_=st_s)))
                def _dbg2(e, inc):
                    for hc_ in range(HC):
                        inc(e.dma_start(out=dbg_x2[:, hc_ * 512:(hc_ + 1) * 512], in_=x2c[hc_]))
                    inc(e.dma_start(out=dbg_tl[:, 0:512], in_=tl_s))
                    inc(e.dma_start(out=dbg_tl[:, 512:1024], in_=t2_s))
                dstep("sync", r_x2 + [r_tl, r_t2], [], _dbg2, n_dma=HC + 2)
            v_out = out_sem_total[0]
            b.step("sync", "LD", 0, [], [],
                   lambda e: e.wait_ge(SH["OT"], v_out))
        except _Done:
            pass

        with nc.Block() as block:
            def replay(e, eng):
                for (emit_w, body, semn, inc) in b.q[eng]:
                    for s, c in emit_w.items():
                        e.wait_ge(SH[s], c)
                    body(e)

            @block.sync
            def _(e):
                replay(e, "sync")

            @block.gpsimd
            def _(e):
                replay(e, "gpsimd")

            @block.vector
            def _(e):
                replay(e, "vector")

            @block.scalar
            def _(e):
                replay(e, "scalar")

            @block.tensor
            def _(e):
                replay(e, "tensor")

    return nc


def _core_tables(p):
    A = list(range(4 * p, 4 * p + 4))
    Bb = list(range(28 - 4 * p, 32 - 4 * p))
    own = A + Bb

    def owner_bl(ub):
        if ub < 16:
            r = ub // 4
            bl = ub - 4 * r
        else:
            r = 7 - ub // 4
            bl = ub - (28 - 4 * r) + 4
        return r, bl

    ab_lo = 4 * p
    lo = [ab_lo + k for k in range(4)] + list(range(0, ab_lo)) + [None] * (12 - ab_lo)
    ab_hi = 28 - 4 * p
    hi = [ab_hi + k for k in range(4)] + list(range(0, ab_hi)) + [None] * (4 * p)
    slots = lo + hi
    assert len(lo) == NSL_LO and len(hi) == NSL_HI

    kidx = np.zeros((128, G * NSLOT), np.int32)
    bias = np.zeros((128, NSLOT), np.float32)
    u = np.arange(128)
    for s, ub in enumerate(slots):
        if ub is None:
            row = 0
            bias[:, s] = NEGB
        else:
            r, bl = owner_bl(ub)
            row = (r * G * 8 + bl) * 128
            bias[:, s] = -SHIFT
        for g in range(G):
            base = row + (0 if ub is None else g * 8 * 128)
            kidx[:, g * NSLOT + s] = base + u
    return own, kidx, bias


def kernel(L0_attn_hidden_states, L0_attn_Wq, L0_attn_bq, L0_attn_Wk, L0_attn_bk,
           L0_attn_Wv, L0_attn_bv, L0_attn_Wo, L0_attn_bo, L0_ffn_w_up_codes,
           L0_ffn_w_up_absmax, L0_ffn_b_up, L0_ffn_w_up_lora_a, L0_ffn_w_up_lora_b,
           L0_ffn_w_down_codes, L0_ffn_w_down_absmax, L0_ffn_b_down,
           L0_ffn_w_down_lora_a, L0_ffn_w_down_lora_b):
    x = np.ascontiguousarray(L0_attn_hidden_states)
    half = 64
    inv = 1.0 / 10000.0 ** (np.arange(half) / half)
    masks = np.zeros((128, 4, 512), np.float32)
    uu = np.arange(128)[:, None]
    jj = np.arange(512)[None, :]
    for k in range(4):
        masks[:, k, :] = (jj >= 128 * k + uu).astype(np.float32)
    masks = masks.reshape(128, 4 * 512).astype(NPBF)
    ident = np.eye(128, dtype=NPBF)

    cod_u_full = L0_ffn_w_up_codes.reshape(HID, D)
    amx_u_full = L0_ffn_w_up_absmax.reshape(HID, 64)
    cod_d_full = L0_ffn_w_down_codes.reshape(D, HID)
    amx_d_full = L0_ffn_w_down_absmax.reshape(D, 172)

    in_maps = []
    meta = []
    for c in range(8):
        bb, p = c // 4, c % 4
        own, kidx, bias = _core_tables(p)
        rows = []
        for ub in own:
            rows.extend(range(ub * 128, ub * 128 + 128))
        rows = np.array(rows)
        x_own = np.ascontiguousarray(x[bb, rows, :]).astype(np.float32)
        x_ownT = np.ascontiguousarray(x_own.T)
        ang = rows.astype(np.float64)[None, :] * inv[:, None]
        cos1 = np.cos(ang)
        sin1 = np.sin(ang)
        cosq = np.concatenate([cos1, cos1], 0).astype(NPBF)
        sinq = np.concatenate([-sin1, sin1], 0).astype(NPBF)

        g0 = sum(UP_N[:c])
        n = UP_N[c]
        cu = np.zeros((UP_PAD * 128, D), np.int32)
        au = np.zeros((UP_PAD * 128, 64), np.float32)
        cu[:n * 128] = cod_u_full[g0 * 128:(g0 + n) * 128]
        au[:n * 128] = amx_u_full[g0 * 128:(g0 + n) * 128]

        im = {
            "x_own": x_own, "x_ownT": x_ownT,
            "Wq": L0_attn_Wq, "Wk": L0_attn_Wk, "Wv": L0_attn_Wv, "Wo": L0_attn_Wo,
            "bq_i": np.ascontiguousarray(L0_attn_bq.reshape(32, 128).T),
            "bk_i": np.ascontiguousarray(L0_attn_bk.reshape(8, 128).T),
            "bv_i": L0_attn_bv.reshape(1, 1024),
            "bo_i": np.ascontiguousarray(L0_attn_bo.reshape(32, 128).T),
            "bd_i": np.ascontiguousarray(L0_ffn_b_down.reshape(32, 128).T),
            "bu_i": np.ascontiguousarray(L0_ffn_b_up.reshape(HC, 128).T),
            "cosq": cosq, "sinq": sinq, "masks_i": masks, "ident_i": ident,
            "btbl_i": bias, "kidx_i": kidx,
            "cod_u": cu, "amx_u": au,
            "cod_d": np.ascontiguousarray(cod_d_full[c * 512:(c + 1) * 512]),
            "amx_d": np.ascontiguousarray(amx_d_full[c * 512:(c + 1) * 512]),
            "la_u": L0_ffn_w_up_lora_a, "lb_u": L0_ffn_w_up_lora_b,
            "la_d": L0_ffn_w_down_lora_a, "lb_d": L0_ffn_w_down_lora_b,
        }
        in_maps.append(im)
        meta.append((bb, rows))

    nc = build_nc()
    res = run_bass_kernel_spmd(nc, in_maps, core_ids=list(range(8)))

    out = np.zeros((BB, S, D), np.float32)
    for c in range(8):
        bb, rows = meta[c]
        shard = res.results[c]["out_fm"]
        out[bb, rows[0:512], :] = shard[0:D, :].T
        out[bb, rows[512:1024], :] = shard[D:2 * D, :].T
    return out
